# version 64
# baseline (speedup 1.0000x reference)
import numpy as np
import concourse.bass as bass
import concourse.mybir as mybir
from concourse.bass_utils import run_bass_kernel_spmd
from contextlib import ExitStack

F32 = mybir.dt.float32
BF16 = mybir.dt.bfloat16
AF = mybir.ActivationFunctionType
ALU = mybir.AluOpType

NB = 2
E = 1024
L = 2048
DCH = 8
NQT = 4
QT = 512
NQH = 2
QH = 256
KC = 16
NPAIR = 8
EPS = 1e-5


class Sem:
    __slots__ = ("h", "n")


DBG = False


def build():
    nc = bass.Bass("TRN2", target_bir_lowering=False)
    dram_in = lambda n, sh: nc.dram_tensor(n, sh, F32, kind="ExternalInput").ap()
    xT = dram_in("xT", [NB, E, L])
    wTs = {n: dram_in(n, [E, E]) for n in ("wq", "wk", "wv", "wo", "wqf", "wout")}
    routersT = dram_in("routersT", [E, 256])
    lv = dram_in("lv", [256, E])
    bmat = dram_in("bmat", [512, 256])
    bqD = dram_in("bq", [128, 8])
    bkD = dram_in("bk", [128, 8])
    boD = dram_in("bo", [128, 8])
    wbD = dram_in("wb", [128, 8])
    bvD = dram_in("bv", [1, E])
    cosD = dram_in("cosT", [128, L])
    sinD = dram_in("sinT", [128, L])
    pswD = dram_in("pswap", [128, 128])
    oT = nc.dram_tensor("oT", [NB, E, L], F32, kind="ExternalOutput").ap()
    dbg = {}
    if DBG:
        for n, sh in (("A", [128, 4096]), ("vp", [128, 16640]), ("qs", [128, 16384]),
                      ("ks", [128, 16384]), ("xn", [128, 16384]), ("ao", [128, 4096]),
                      ("x1", [128, 2048]), ("r2a", [1, 256]), ("r2b", [1, 256]), ("xn1", [128, 2048]), ("q2", [128, 2048]),
                      ("lg", [128, 1024]), ("lp", [128, 512]), ("rt", [128, 2048]),
                      ("o2", [128, 2048])):
            dbg[n] = nc.dram_tensor("dbg_" + n, sh, F32, kind="ExternalOutput").ap()
    scr = nc.dram_tensor("scr", [512, 512], F32, kind="Internal").ap()
    wstage = nc.dram_tensor("wstage", [6, 128, 8192], BF16, kind="Internal").ap()
    WIDX = {"wq": 0, "wk": 1, "wv": 2, "wo": 3, "wqf": 4, "wout": 5}

    ctx = ExitStack()
    with ctx:
        sb = lambda n, sh, dt: ctx.enter_context(nc.sbuf_tensor(n, sh, dt))
        q_sb = sb("q_sb", [128, 16384], BF16)
        k_sb = sb("k_sb", [128, 16384], BF16)
        vp = sb("vp", [128, 16640], BF16)
        ws0 = sb("ws0", [128, 8192], BF16)
        ws1 = sb("ws1", [128, 8192], BF16)
        poolA = sb("poolA", [128, 16384], BF16)
        poolB = sb("poolB", [128, 8192], BF16)
        expb = sb("expb", [128, 2048], BF16)
        sqb = sb("sqb", [128, 2048], BF16)
        rw = sb("rw", [128, 1024], F32)
        rsnm = sb("rsnm", [128, 512], F32)
        recBb = sb("recBb", [128, 512], BF16)
        ttmp = sb("ttmp", [128, 512], F32)
        tab = sb("tab", [128, 4096], BF16)
        bmf = sb("bmf", [128, 1024], F32)
        pswb = sb("pswb", [128, 128], BF16)
        onesb = sb("onesb", [128, 1], BF16)
        onesrow = sb("onesrow", [1, 128], BF16)
        epsc = sb("epsc", [128, 1], F32)
        onef = sb("onef", [128, 1], F32)
        onesrow_f = sb("onesrow_f", [1, 128], F32)
        invE = sb("invE", [128, 1], BF16)
        invEf = sb("invEf", [128, 1], F32)
        bqF = sb("bqF", [128, 8], F32)
        bkF = sb("bkF", [128, 8], F32)
        boF = sb("boF", [128, 8], F32)
        wbF = sb("wbF", [128, 8], F32)

        psAB = ctx.enter_context(nc.psum_tensor("psAB", [128, 2048], F32))
        psA = psAB[:, 0:1024]
        psB = psAB[:, 1024:2048]
        psE = ctx.enter_context(nc.psum_tensor("psE", [128, 512], F32))
        psO = ctx.enter_context(nc.psum_tensor("psO", [128, 512], F32))
        psE2 = ctx.enter_context(nc.psum_tensor("psE2", [128, 512], F32))
        psO2 = ctx.enter_context(nc.psum_tensor("psO2", [128, 512], F32))

        poolA32 = poolA[:].bitcast(F32)
        poolB32 = poolB[:].bitcast(F32)
        ttmpb = ttmp[:].bitcast(BF16)

        sems = {}
        for n in ("x", "w", "g", "mm", "ex", "ve", "bw", "bb", "st", "mbw", "mbb",
                  "xr0", "xr1", "wl", "tab", "bv", "st1", "gc"):
            s = Sem()
            s.h = ctx.enter_context(nc.semaphore(n))
            s.n = 0
            sems[n] = s
        (sx, sw, sg, smm, sex, sve, sbw, sbb, sst, smbw, smbb, sxr0, sxr1,
         swl, stab, sbv, sst1, sgc) = (
            sems[n] for n in ("x", "w", "g", "mm", "ex", "ve", "bw", "bb", "st",
                              "mbw", "mbb", "xr0", "xr1", "wl", "tab", "bv", "st1",
                              "gc"))
        sxr = (sxr0, sxr1)
        sstp = (sst, sst1)

        P = {k: [] for k in "tvsgy"}

        import traceback as _tb

        def em(eng, waits, buildf, inc=None, incby=None):
            src = _tb.extract_stack()[-2]

            def f(e, waits=tuple(waits), buildf=buildf, inc=inc, src=src):
                try:
                    for (h, v) in waits:
                        if v > 0:
                            e.wait_ge(h, v)
                    ins = buildf(e)
                    if inc is not None:
                        ins.then_inc(inc[0], inc[1])
                except Exception:
                    print("EMIT-FAIL at kernel.py line", src.lineno, ":", src.line)
                    raise
            P[eng].append(f)

        def W(s):
            return (s.h, s.n)

        def I(s, k=1):
            s.n += k
            return (s.h, k)

        P4SLOT = (psE2[:, 0:256], psO2[:, 0:256])
        EST_M = None

        em('g', [], lambda g: g.memset(
            vp[:, 0:16512].rearrange("p (k h c) -> p k h c", k=16, h=8)[:, :, :, 64:65],
            1.0), inc=I(sg, 16))
        em('g', [], lambda g: g.memset(onesb[:], 1.0), inc=I(sg, 16))
        em('g', [], lambda g: g.memset(onesrow[:], 1.0), inc=I(sg, 16))
        em('g', [], lambda g: g.memset(epsc[:], EPS), inc=I(sg, 16))
        em('g', [], lambda g: g.memset(onef[:], 1.0), inc=I(sg, 16))
        em('g', [], lambda g: g.memset(onesrow_f[:], 1.0), inc=I(sg, 16))
        em('g', [], lambda g: g.memset(invE[:], 1.0 / E), inc=I(sg, 16))
        em('g', [], lambda g: g.memset(invEf[:], 1.0 / E), inc=I(sg, 16))
        g_init = W(sg)
        em('g', [], lambda g: g.dma_start(out=pswb[:], in_=pswD[:]), inc=I(sw, 16))
        em('y', [], lambda y: y.dma_start(
            out=bmf[:].rearrange("p (k m) -> p k m", k=4),
            in_=bmat.rearrange("(k p) m -> p k m", p=128)), inc=I(sx, 16))
        for t_sb, t_d in ((bqF, bqD), (bkF, bkD), (boF, boD), (wbF, wbD)):
            em('y', [], lambda y, a=t_sb, b=t_d: y.dma_start(out=a[:], in_=b[:]),
               inc=I(sx, 16))
        const_x = W(sx)
        for i, name in enumerate(("wq", "wk", "wv", "wo", "wqf", "wout")):
            em('g', [W(sbw)], lambda g, n=name: g.dma_start(
                out=ws0[:].rearrange("p (d e) -> p d e", d=8),
                in_=wTs[n].rearrange("(d p) e -> p d e", p=128)),
                inc=I(sw, 16))
            em('y', [W(sw)], lambda y, i=i: y.dma_start(
                out=wstage[i], in_=ws0[:]), inc=I(sbw, 16))
        stage_done = W(sbw)

        def load_w(slot, name, waits):
            em('y', [stage_done] + waits, lambda y, s=slot, i=WIDX[name]:
               y.dma_start(out=s[:], in_=wstage[i]), inc=I(swl, 16))
            return W(swl)

        scr_next = [0]

        def scr_slot():
            s = scr_next[0]
            scr_next[0] += 1
            assert s < 510
            return scr[s:s + 1, :]

        def rows_stats_gen(ps_mean, ps_msq, waits_mm, width=256):
            if width == 256:
                s_mean, s_rst = rw[0:1, 0:256], rw[0:1, 256:512]
                s_scr, s_nm = rw[0:1, 512:768], rw[0:1, 768:1024]
            else:
                s_mean, s_scr = rw[0:1, 0:512], rw[0:1, 512:1024]
                s_rst, s_nm = ttmp[0:1, 0:512], rw[0:1, 512:1024]
            em('v', waits_mm, lambda v: v.tensor_copy(s_mean, ps_mean), inc=I(sve))
            em('v', [], lambda v: v.tensor_tensor(s_scr, s_mean, s_mean, ALU.mult),
               inc=I(sve))
            em('v', [], lambda v: v.tensor_tensor(s_rst, ps_msq, s_scr, ALU.subtract),
               inc=I(sve))
            psum_free = W(sve)
            var_done = W(sve)
            for _ in range(6):
                yield None
            em('s', [var_done, W(sex)], lambda s: s.activation(
                s_scr, s_rst, AF.Ln, bias=epsc[0:1, 0:1]), inc=I(sex))
            ln_done = W(sex)
            yield None
            yield None
            em('s', [ln_done], lambda s: s.activation(s_rst, s_scr, AF.Exp, scale=-0.5),
               inc=I(sex))
            rstd_done = W(sex)
            em('v', [rstd_done], lambda v: v.scalar_tensor_tensor(
                s_nm, s_mean, -1.0, s_rst, ALU.mult, ALU.mult), inc=I(sve))
            return psum_free, W(sve), s_rst, s_nm

        def rows_stats(ps_mean, ps_msq, waits_mm, width=256):
            g = rows_stats_gen(ps_mean, ps_msq, waits_mm, width)
            while True:
                try:
                    next(g)
                except StopIteration as e:
                    return e.value

        def bcast_rows(rows_done, s_rst, s_nm, width=256, dst=None):
            if dst is None:
                dst = (psE[:, 0:256], psE[:, 256:512])
            if not isinstance(rows_done, list):
                rows_done = [rows_done]
            em('t', rows_done, lambda t: t.matmul(
                dst[0], onesrow_f[0:1, 0:128], s_rst,
                start=True, stop=True, skip_group_check=True), inc=None)
            em('t', [], lambda t: t.matmul(
                dst[1], onesrow_f[0:1, 0:128], s_nm,
                start=True, stop=True, skip_group_check=True), inc=I(smm))
            return W(smm)

        def ebounce_rows(rows_done, s_rst, s_nm, rsnm_free):
            em('t', [rows_done], lambda t, s_rst=s_rst: t.matmul(
                psE2[:, 256:512], onesrow_f[0:1, 0:128], s_rst,
                start=True, stop=True, skip_group_check=True), inc=None)
            em('t', [], lambda t, s_nm=s_nm: t.matmul(
                psO2[:, 256:512], onesrow_f[0:1, 0:128], s_nm,
                start=True, stop=True, skip_group_check=True), inc=I(smm))
            return W(smm)

        def stat_mms(src_x, src_sq, first_waits, sq_waits, ps_m=None, ps_q=None,
                     inv=None):
            if ps_m is None:
                ps_m, ps_q = psA[0:1, 0:256], psA[0:1, 512:768]
            if inv is None:
                inv = invE
            for c in range(DCH):
                w = list(first_waits(c)) if first_waits else []
                em('t', w, lambda t, c=c: t.matmul(
                    ps_m, inv[:, 0:1], src_x(c),
                    start=(c == 0), stop=(c == DCH - 1), skip_group_check=True),
                    inc=None)
            for c in range(DCH):
                w = list(sq_waits(c)) if sq_waits else []
                em('t', w, lambda t, c=c: t.matmul(
                    ps_q, invE[:, 0:1], src_sq(c),
                    start=(c == 0), stop=(c == DCH - 1), skip_group_check=True),
                    inc=I(smm) if c == DCH - 1 else None)
            return W(smm)

        pend = []
        T = None

        prev_psA_rows = W(sve)
        prev_sqb_free = W(smm)
        prev_expb_free = W(smm)
        last_out2_mm = W(smm)
        last_op_mm = W(smm)
        T = {
            'exp_set': [W(sex), W(sex)],
            'v_half': [W(smm), W(smm)],
            'norm': W(sve),
            'recB_free': W(sve),
            'rw64_free': W(sve),
            'rw32_free': W(sex),
            'mid_bw': W(smbw),
            'A_free': {0: W(smm), 1: W(smm)},
            'A_ready': {},
            'rsnm_free': W(sve),
            'psO2_free': W(sve),
            'slot': [W(sve), W(sve)],
            'sqb_free': prev_sqb_free,
            'aob_free': W(smm),
            'ws0_free': last_out2_mm,
            'q2_free': W(smm),
            'x2r_free': [W(sst), W(sst1)],
            'op_flag': {},
            'ao_evac': W(sve),
        }

        for b in range(NB):
            lw_q = load_w(ws0, "wq", [last_out2_mm])
            lw_k = load_w(ws1, "wk", [last_op_mm])

            xn_caps = []
            sq_caps = []
            for qt5 in range(4):
                t0 = qt5 * 512
                scr5 = vp[:, qt5 * 4096:(qt5 + 1) * 4096]
                xsrc = xT[b].rearrange("(d p) t -> p d t", p=128)[:, :, t0:t0 + 512]
                semx = sw if qt5 % 2 == 0 else sx
                if qt5 == 0:
                    cw = [T['v_half'][0], T['v_half'][1]]
                elif qt5 >= 2:
                    cw = [sq_caps[qt5 - 2]]
                else:
                    cw = []
                em('g', cw, lambda g, scr5=scr5, xsrc=xsrc: g.dma_start(
                    out=scr5.rearrange("p (d t) -> p d t", d=8), in_=xsrc),
                    inc=I(semx, 16))
                x_ready = W(semx)
                em('v', [x_ready, prev_sqb_free, prev_expb_free],
                   lambda v, scr5=scr5: v.tensor_tensor(
                       sqb[:, 0:2048], scr5[:, 0:2048], scr5[:, 0:2048],
                       ALU.mult), inc=I(sve))
                em('v', [], lambda v, scr5=scr5: v.tensor_tensor(
                    expb[:, 0:2048], scr5[:, 2048:4096], scr5[:, 2048:4096],
                    ALU.mult), inc=I(sve))
                sq_ready = W(sve)
                sq_caps.append(sq_ready)
                for d in range(DCH):
                    w = ([x_ready, prev_psA_rows, g_init,
                          T['exp_set'][0], T['exp_set'][1]] if d == 0 else [])
                    em('t', w, lambda t, d=d, scr5=scr5: t.matmul(
                        psA[0:1, 0:512], invE[:, 0:1],
                        scr5[:, d * 512:(d + 1) * 512],
                        start=(d == 0), stop=(d == DCH - 1),
                        skip_group_check=True), inc=None)
                for d in range(DCH):
                    src = (sqb if d < 4 else expb)
                    off = (d % 4) * 512
                    w = [sq_ready] if d == 0 else []
                    em('t', w, lambda t, d=d, src=src, off=off: t.matmul(
                        psA[0:1, 512:1024], invE[:, 0:1], src[:, off:off + 512],
                        start=(d == 0), stop=(d == DCH - 1),
                        skip_group_check=True),
                        inc=I(smm) if d == DCH - 1 else None)
                stats = W(smm)
                prev_sqb_free = stats
                prev_expb_free = stats
                psum_free, rows_done, rst_r, nm_r = rows_stats(
                    psA[0:1, 0:512], psA[0:1, 512:1024], [stats], width=512)
                prev_psA_rows = psum_free
                bw = [rows_done] if qt5 == 0 else [rows_done, xn_caps[qt5 - 1]]
                bbc = bcast_rows(bw, rst_r, nm_r, width=512,
                                 dst=(psE[:, 0:512], psO[:, 0:512]))
                for d in range(DCH):
                    w = []
                    if d == 0:
                        w = [bbc]
                        if qt5 == 0:
                            w += [T['x2r_free'][0], T['x2r_free'][1]]
                    em('v', w, lambda v, d=d, scr5=scr5: v.tensor_tensor(
                        ttmp[:, 0:512], scr5[:, d * 512:(d + 1) * 512],
                        psE[:, 0:512], ALU.mult), inc=I(sve))
                    em('v', [], lambda v, d=d, t0=t0: v.tensor_tensor(
                        poolA[:, d * 2048 + t0: d * 2048 + t0 + 512],
                        ttmp[:, 0:512], psO[:, 0:512], ALU.add), inc=I(sve))
                xn_caps.append(W(sve))
            xn_done = xn_caps[-1]
            em('g', [W(smm), W(sve)], lambda g: g.dma_start(
                out=tab[:, 0:2048], in_=cosD[:]), inc=I(stab, 16))
            em('g', [], lambda g: g.dma_start(
                out=tab[:, 2048:4096], in_=sinD[:]), inc=I(stab, 16))
            tab_ready = W(stab)
            em('g', [xn_done], lambda g: g.memset(
                vp[:, 0:16512].rearrange(
                    "p (k h c) -> p k h c", k=16, h=8)[:, :, :, 64:65],
                1.0), inc=I(sg, 16))
            ones_ready = W(sg)

            def qkv_qk(dst, wslot, lw, biasF):
                idx = [(c, qt) for qt in range(NQT) for c in range(DCH)]
                evac_done = {}
                swap_done = {}
                rope_done = {}
                caps = []

                def emit_swap_rope(j):
                    c, qt = idx[j]
                    slot = j % 2
                    psw = psB[:, slot * 512:(slot + 1) * 512]
                    sw_waits = [evac_done[j], tab_ready]
                    if j >= 2:
                        sw_waits.append(rope_done[j - 2])
                    else:
                        sw_waits.append(W(sex))
                    em('t', sw_waits, lambda t, c=c, qt=qt, psw=psw: t.matmul(
                        psw, pswb[:, 0:128],
                        dst[:, c * 2048 + qt * 512: c * 2048 + (qt + 1) * 512],
                        start=True, stop=True, skip_group_check=True), inc=I(smm))
                    swap_done[j] = W(smm)
                    em('v', [swap_done[j]], lambda v, qt=qt, psw=psw: v.tensor_tensor(
                        ttmpb[:, 0:512], psw,
                        tab[:, 2048 + qt * 512: 2048 + (qt + 1) * 512],
                        ALU.mult), inc=I(sve))
                    em('v', [], lambda v, c=c, qt=qt: v.tensor_tensor(
                        ttmpb[:, 512:1024],
                        dst[:, c * 2048 + qt * 512: c * 2048 + (qt + 1) * 512],
                        tab[:, qt * 512:(qt + 1) * 512], ALU.mult), inc=I(sve))
                    em('v', [], lambda v, c=c, qt=qt: v.tensor_tensor(
                        dst[:, c * 2048 + qt * 512: c * 2048 + (qt + 1) * 512],
                        ttmpb[:, 0:512], ttmpb[:, 512:1024], ALU.add), inc=I(sve))
                    rope_done[j] = W(sve)

                for i, (c, qt) in enumerate(idx):
                    slot = i % 2
                    ps = psA[:, slot * 512:(slot + 1) * 512]
                    for d in range(DCH):
                        waits = []
                        if d == 0:
                            waits = [lw, xn_caps[qt]]
                            if i >= 2:
                                waits.append(evac_done[i - 2])
                        em('t', waits, lambda t, c=c, qt=qt, d=d, ps=ps: t.matmul(
                            ps, wslot[:, d * 1024 + c * 128: d * 1024 + (c + 1) * 128],
                            poolA[:, d * 2048 + qt * 512: d * 2048 + (qt + 1) * 512],
                            start=(d == 0), stop=(d == DCH - 1),
                            skip_group_check=True),
                            inc=I(smm) if d == DCH - 1 else None)
                    mm_done = W(smm)
                    em('v', [mm_done], lambda v, c=c, qt=qt, ps=ps: v.tensor_scalar(
                        dst[:, c * 2048 + qt * 512: c * 2048 + (qt + 1) * 512],
                        ps, biasF[:, c:c + 1], None, ALU.add), inc=I(sve))
                    evac_done[i] = W(sve)
                    if i > 0:
                        emit_swap_rope(i - 1)
                emit_swap_rope(len(idx) - 1)
                pos = {cq: j for j, cq in enumerate(idx)}
                for c in range(DCH):
                    for qt in range(NQT):
                        j = pos[(c, qt)]
                        caps.append((swap_done[j], rope_done[j]))
                return caps

            q_rope = qkv_qk(q_sb, ws0, lw_q, bqF)
            lw_v = load_w(ws0, "wv", [q_rope[-1][0]])
            k_rope = qkv_qk(k_sb, ws1, lw_k, bkF)
            lw_o = load_w(ws1, "wo", [k_rope[-1][0]])

            rwb = rw[:].bitcast(BF16)
            em('g', [W(sve), W(smm)], lambda g, rwb=rwb: g.dma_start(
                out=rwb[0:1, 0:512], in_=bvD[0:1, 0:512]), inc=I(sbv, 16))
            em('g', [], lambda g, rwb=rwb: g.dma_start(
                out=rwb[0:1, 512:1024], in_=bvD[0:1, 512:1024]), inc=I(sbv, 16))
            bv_ready = W(sbv)
            v_evac = []
            for kt in range(KC):
                for eh in range(2):
                    ps = (psE if (kt * 2 + eh) % 2 == 0 else psO)[:, 0:512]
                    for d in range(DCH):
                        waits = []
                        if d == 0:
                            waits = [lw_v, xn_done, g_init]
                            if len(v_evac) >= 2:
                                waits.append(v_evac[-2])
                        em('t', waits, lambda t, kt=kt, d=d, eh=eh, ps=ps: t.matmul(
                            ps, poolA[:, d * 2048 + kt * 128: d * 2048 + (kt + 1) * 128],
                            ws0[:, d * 1024 + eh * 512: d * 1024 + (eh + 1) * 512],
                            start=(d == 0), stop=False, skip_group_check=True),
                            inc=None)
                    em('t', [bv_ready] if len(v_evac) == 0 else [],
                       lambda t, eh=eh, ps=ps: t.matmul(
                           ps, onesrow[0:1, 0:128],
                           rw[:].bitcast(BF16)[0:1, eh * 512:(eh + 1) * 512],
                           start=False, stop=True, skip_group_check=True), inc=I(smm))
                    vmm = W(smm)
                    base = lambda kt, eh: kt * 1032 + eh * 4 * 129
                    em('v', [vmm, ones_ready] if len(v_evac) == 0 else [vmm],
                       lambda v, kt=kt, eh=eh, ps=ps, base=base:
                       v.tensor_copy(
                           vp[:, base(kt, eh): base(kt, eh) + 4 * 129]
                           .rearrange("p (pr c) -> p pr c", pr=4)[:, :, 0:64],
                           ps.rearrange("p (h c) -> p h c", h=8)[:, 0::2, :]),
                       inc=I(sve))
                    em('v', [], lambda v, kt=kt, eh=eh, ps=ps, base=base:
                       v.tensor_copy(
                           vp[:, base(kt, eh) + 65: base(kt, eh) + 65 + 4 * 129]
                           .rearrange("p (pr c) -> p pr c", pr=4)[:, :, 0:64],
                           ps.rearrange("p (h c) -> p h c", h=8)[:, 1::2, :]),
                       inc=I(sve))
                    v_evac.append(W(sve))
            v_done = v_evac[-1]
            last_out2_mm = W(smm)

            em('g', [q_rope[-1][0], k_rope[-1][0], W(sve)],
               lambda g: g.dma_start(
                   out=tab[:, 0:2048].rearrange("p (l e) -> p l e", l=2),
                   in_=lv.rearrange("(l p) e -> p l e", p=128)), inc=I(stab, 16))
            em('g', [], lambda g: g.dma_start(
                out=tab[:, 2048:4096].rearrange("p (d r) -> p d r", d=8),
                in_=routersT.rearrange("(d p) r -> p d r", p=128)), inc=I(stab, 16))
            ffn_tab = W(stab)

            def Abuf(qt):
                p = qt % 2
                return poolB[:, p * 4096:(p + 1) * 4096]

            def middle_gen(qt):
                steps = [(c, kc) for c in range(NPAIR) for kc in range(KC)]
                scores_done = {}
                mlazy = []

                def mpump(n):
                    for _ in range(n):
                        while mlazy:
                            try:
                                next(mlazy[0])
                                break
                            except StopIteration:
                                mlazy.pop(0)
                        if not mlazy:
                            return

                def emit_scores(i):
                    c, kc = steps[i]
                    sset = kc % 2
                    sc = psA if sset == 0 else psB
                    waits = [T['exp_set'][sset]]
                    if kc == 0:
                        waits.append(q_rope[c * NQT + qt][1])
                        if c == 0 and qt == 0:
                            waits += [k_rope[-1][1], v_done]
                    em('t', waits, lambda t, c=c, kc=kc, qt=qt, sc=sc: t.matmul(
                        sc[:, 0:512],
                        k_sb[0:64, c * 2048 + kc * 128: c * 2048 + (kc + 1) * 128],
                        q_sb[0:64, c * 2048 + qt * 512: c * 2048 + (qt + 1) * 512],
                        start=True, stop=True, skip_group_check=True), inc=None)
                    em('t', [], lambda t, c=c, kc=kc, qt=qt, sc=sc: t.matmul(
                        sc[:, 512:1024],
                        k_sb[64:128, c * 2048 + kc * 128: c * 2048 + (kc + 1) * 128],
                        q_sb[64:128, c * 2048 + qt * 512: c * 2048 + (qt + 1) * 512],
                        start=True, stop=True, skip_group_check=True), inc=I(smm))
                    scores_done[i] = W(smm)

                emit_scores(0)
                emit_scores(1)
                for i, (c, kc) in enumerate(steps):
                    sset = kc % 2
                    sc = psA if sset == 0 else psB
                    half = kc % 2
                    if i % 2 == 0:
                        em('s', [scores_done[i], scores_done[i + 1],
                                 T['v_half'][0], T['v_half'][1]],
                           lambda s: s.activation(
                               expb[:, 0:2048], psAB[:, 0:2048],
                               AF.Exp, scale=0.125), inc=I(sex))
                        cap = W(sex)
                        T['exp_set'][0] = cap
                        T['exp_set'][1] = cap
                        if i + 2 < len(steps):
                            emit_scores(i + 2)
                        if i + 3 < len(steps):
                            emit_scores(i + 3)
                    exp_done = T['exp_set'][sset]
                    vw = [exp_done]
                    if kc == 0:
                        vw.append(T['norm'])
                    em('t', vw, lambda t, c=c, kc=kc, half=half: t.matmul(
                        psE[0:65, :],
                        vp[:, kc * 1032 + c * 129: kc * 1032 + c * 129 + 65],
                        expb[:, half * 1024: half * 1024 + 512],
                        start=(kc == 0), stop=(kc == KC - 1),
                        skip_group_check=True), inc=None)
                    em('t', [], lambda t, c=c, kc=kc, half=half: t.matmul(
                        psO[64:128, :],
                        vp[:, kc * 1032 + c * 129 + 65: kc * 1032 + c * 129 + 129],
                        expb[:, half * 1024 + 512: half * 1024 + 1024],
                        start=(kc == 0), stop=(kc == KC - 1),
                        tile_position=(0, 64), skip_group_check=True), inc=None)
                    em('t', [], lambda t, kc=kc, half=half: t.matmul(
                        psO[32:33, :], onesb[:, 0:1],
                        expb[:, half * 1024 + 512: half * 1024 + 1024],
                        start=(kc == 0), stop=(kc == KC - 1),
                        tile_position=(0, 32), skip_group_check=True), inc=I(smm))
                    T['v_half'][half] = W(smm)
                    if kc == KC - 1:
                        while mlazy:
                            mpump(1)
                        pair_mm = W(smm)
                        em('s', [pair_mm, T['rw64_free']], lambda s: s.copy(
                            out=rw[64:65, 512:1024], in_=psE[64:65, :]), inc=I(sex))
                        em('s', [T['rw32_free']], lambda s: s.copy(
                            out=rw[32:33, 512:1024], in_=psO[32:33, :]), inc=I(sex))
                        rows_cap = W(sex)
                        em('s', [T['A_free'][qt % 2]] if c < 2 else [],
                           lambda s, c=c, qt=qt: s.copy(
                               out=Abuf(qt)[0:64, c * 512:(c + 1) * 512],
                               in_=psE[0:64, :]), inc=I(sex))
                        em('s', [], lambda s, c=c, qt=qt: s.copy(
                            out=Abuf(qt)[64:128, c * 512:(c + 1) * 512],
                            in_=psO[64:128, :]), inc=I(sex))
                        T['norm'] = W(sex)

                        def lazy_gen(c=c, qt=qt, rows_cap=rows_cap,
                                     acap=T['norm']):
                            sl_e, sl_o = scr_slot(), scr_slot()
                            em('g', [rows_cap], lambda g, sl_e=sl_e: g.dma_start(
                                out=sl_e[:, 0:512], in_=rw[64:65, 512:1024]),
                                inc=I(smbw, 16))
                            em('g', [], lambda g, sl_o=sl_o: g.dma_start(
                                out=sl_o[:, 0:512], in_=rw[32:33, 512:1024]),
                                inc=I(smbw, 16))
                            T['mid_bw'] = W(smbw)
                            T['rw64_free'] = T['mid_bw']
                            T['rw32_free'] = T['mid_bw']
                            yield
                            em('g', [T['mid_bw'], T['recB_free']],
                               lambda g, sl_e=sl_e: g.dma_start(
                                   out=recBb[0:64, 0:512],
                                   in_=sl_e[:, 0:512].to_broadcast((64, 512))),
                               inc=I(smbb, 16))
                            em('g', [], lambda g, sl_o=sl_o: g.dma_start(
                                out=recBb[64:128, 0:512],
                                in_=sl_o[:, 0:512].to_broadcast((64, 512))),
                                inc=I(smbb, 16))
                            bbr = W(smbb)
                            yield
                            def _recip(v):
                                with nc.allow_low_precision(
                                        reason="bf16 denom bcast (matches prior "
                                               "bf16-bounced reciprocal rows)"):
                                    return v.reciprocal(
                                        recBb[:, 0:512], recBb[:, 0:512])
                            em('v', [bbr], _recip, inc=I(sve))
                            yield
                            em('v', [acap], lambda v, c=c, qt=qt: v.tensor_tensor(
                                Abuf(qt)[0:64, c * 512:(c + 1) * 512],
                                Abuf(qt)[0:64, c * 512:(c + 1) * 512],
                                recBb[0:64, 0:512], ALU.mult), inc=I(sve))
                            em('v', [], lambda v, c=c, qt=qt: v.tensor_tensor(
                                Abuf(qt)[64:128, c * 512:(c + 1) * 512],
                                Abuf(qt)[64:128, c * 512:(c + 1) * 512],
                                recBb[64:128, 0:512], ALU.mult), inc=I(sve))
                            T['recB_free'] = W(sve)
                            T['A_ready'][qt] = W(sve)

                        mlazy.append(lazy_gen())
                    mpump(2)
                    yield
                while mlazy:
                    mpump(1)

            def ddump(name, src_ap, wait):
                if DBG:
                    em('g', [wait], lambda g, s_=src_ap, d=dbg[name]: g.dma_start(
                        out=d[:], in_=s_), inc=I(sw, 16))

            def epilogue_gen(qt):
                nonlocal prev_sqb_free, last_out2_mm, last_op_mm
                AB = Abuf(qt)
                aob = lambda c: poolA[:, 12288 + c * 256: 12288 + (c + 1) * 256]
                if DBG and b == 0 and qt == 0:
                    ddump("A", AB, T['A_ready'][qt])
                    ddump("vp", vp[:], T['A_ready'][qt])
                    ddump("qs", q_sb[:], T['A_ready'][qt])
                    ddump("ks", k_sb[:], T['A_ready'][qt])
                    ddump("xn", poolA[:], T['A_ready'][qt])
                for qh in range(NQH):
                    h0 = qh * 256
                    oev = []
                    if True:
                        for c in range(DCH):
                            ps4 = P4SLOT[c % 2]
                            for j in range(DCH):
                                waits = []
                                if j == 0:
                                    waits = [T['A_ready'][qt], lw_o]
                                    if len(oev) >= 2:
                                        waits.append(oev[-2][1])
                                    else:
                                        waits += [T['slot'][c % 2], T['psO2_free']]
                                em('t', waits, lambda t, c=c, j=j, ps4=ps4, h0=h0:
                                   t.matmul(
                                       ps4,
                                       ws1[:, j * 1024 + c * 128: j * 1024 + (c + 1) * 128],
                                       AB[:, j * 512 + h0: j * 512 + h0 + 256],
                                       start=(j == 0), stop=(j == DCH - 1),
                                       skip_group_check=True),
                                   inc=I(smm) if j == DCH - 1 else None)
                            mmc = W(smm)
                            em('v', [mmc, T['aob_free'] if c == 0 else (smm.h, 0)],
                               lambda v, c=c, ps4=ps4, h0=h0: v.tensor_scalar(
                                   poolA[:, c * 512 + h0: c * 512 + h0 + 256], ps4,
                                   boF[:, c:c + 1], None, ALU.add), inc=I(sve))
                            oev.append((W(smm), W(sve)))
                            T['slot'][c % 2] = oev[-1][1]
                            yield
                        T['ao_evac'] = W(sve)
                    aoh = lambda c, h0=h0: poolA[:, c * 512 + h0: c * 512 + h0 + 256]
                    for c in range(DCH):
                        em('v', [T['ao_evac'], T['sqb_free']] if c == 0 else [],
                           lambda v, c=c, aoh=aoh: v.tensor_tensor(
                               sqb[:, c * 256:(c + 1) * 256], aoh(c), aoh(c),
                               ALU.mult), inc=I(sve))
                        if c == 3:
                            yield
                    sq_done = W(sve)
                    if qh == NQH - 1:
                        T['A_free'][qt % 2] = W(smm)
                        T['op_flag'][b * NQT + qt] = True
                    last_op_mm = W(smm)
                    yield
                    yield
                    yield
                    if qh == 0:
                        yield
                        yield
                        yield
                    stats2 = stat_mms(
                        lambda c, aoh=aoh: aoh(c),
                        lambda c: sqb[:, c * 256:(c + 1) * 256],
                        lambda c: [sq_done, T['psO2_free']] if c == 0 else [],
                        None, ps_m=psE2[0:1, 256:512], ps_q=psO2[0:1, 256:512])
                    T['sqb_free'] = stats2
                    T['aob_free'] = stats2
                    prev_sqb_free = stats2
                    psum_free, rows2, rst2, nm2 = rows_stats(
                        psE2[0:1, 256:512], psO2[0:1, 256:512], [stats2])
                    T['psO2_free'] = psum_free
                    bb2 = ebounce_rows(rows2, rst2, nm2, T['rsnm_free'])
                    yield
                    if DBG and b == 0 and qt == 0 and qh == 0:
                        ddump("ao", poolA[:, 0:4096], T['ao_evac'])
                    x1caps = []
                    for c in range(DCH):
                        xrsl = 7680 + (c % 2) * 256
                        em('y', [bb2, T['x2r_free'][c % 2]] if c < 2
                           else [x1caps[c - 2]],
                           lambda y, c=c, xrsl=xrsl, h0=h0, qt=qt, b=b: y.dma_start(
                               out=poolA32[:, xrsl:xrsl + 256],
                               in_=xT[b, c * 128:(c + 1) * 128,
                                      qt * 512 + h0: qt * 512 + h0 + 256]),
                           inc=I(sxr[c % 2], 16))
                        xr_ready = W(sxr[c % 2])
                        em('v', [bb2, xr_ready], lambda v, c=c, aoh=aoh: v.tensor_tensor(
                            ttmp[:, 0:256], aoh(c),
                            psE2[:, 256:512], ALU.mult), inc=I(sve))
                        em('v', [], lambda v: v.tensor_tensor(
                            ttmp[:, 256:512], ttmp[:, 0:256], psO2[:, 256:512],
                            ALU.add), inc=I(sve))
                        em('v', [], lambda v, c=c, xrsl=xrsl: v.tensor_tensor(
                            poolA32[:, 2048 + c * 256: 2048 + (c + 1) * 256],
                            ttmp[:, 256:512], poolA32[:, xrsl:xrsl + 256], ALU.add),
                            inc=I(sve))
                        x1caps.append(W(sve))
                        if c % 3 == 2:
                            yield
                    T['rsnm_free'] = W(sve)
                    yield
                    if DBG and b == 0 and qt == 0 and qh == 0:
                        ddump("x1", poolA32[:, 2048:4096], x1caps[-1])
                        ddump("r2a", rw[0:1, 256:512], rows2)
                        ddump("r2b", rw[0:1, 768:1024], rows2)
                    for c in range(DCH):
                        em('v', [T['sqb_free']] if c == 0 else [],
                           lambda v, c=c: v.tensor_tensor(
                               sqb[:, c * 256:(c + 1) * 256],
                               poolA32[:, 2048 + c * 256: 2048 + (c + 1) * 256],
                               poolA32[:, 2048 + c * 256: 2048 + (c + 1) * 256],
                               ALU.mult), inc=I(sve))
                    ln3_src = W(sve)
                    yield
                    yield
                    yield
                    if qh == 0:
                        yield
                        yield
                    stats3 = stat_mms(
                        lambda c: poolA32[:, 2048 + c * 256: 2048 + (c + 1) * 256],
                        lambda c: sqb[:, c * 256:(c + 1) * 256],
                        lambda c: [ln3_src, T['psO2_free']] if c == 0 else [],
                        None, ps_m=psE2[0:1, 256:512], ps_q=psO2[0:1, 256:512],
                        inv=invEf)
                    T['sqb_free'] = stats3
                    prev_sqb_free = stats3
                    psum_free, rows3, rst3, nm3 = rows_stats(
                        psE2[0:1, 256:512], psO2[0:1, 256:512], [stats3])
                    T['psO2_free'] = psum_free
                    bb3 = ebounce_rows(rows3, rst3, nm3, T['rsnm_free'])
                    yield
                    for c in range(DCH):
                        em('v', [bb3] if c == 0 else [], lambda v, c=c: v.tensor_tensor(
                            ttmp[:, 0:256],
                            poolA32[:, 2048 + c * 256: 2048 + (c + 1) * 256],
                            psE2[:, 256:512], ALU.mult), inc=I(sve))
                        em('v', [], lambda v, c=c: v.tensor_tensor(
                            poolA[:, 8192 + c * 256: 8192 + (c + 1) * 256],
                            ttmp[:, 0:256], psO2[:, 256:512], ALU.add), inc=I(sve))
                        if c % 3 == 2:
                            yield
                    xn1_done = W(sve)
                    T['rsnm_free'] = W(sve)
                    T['psO2_free'] = W(sve)
                    yield
                    if DBG and b == 0 and qt == 0 and qh == 0:
                        ddump("xn1", poolA[:, 8192:10240], xn1_done)
                    yield
                    yield
                    yield
                    lw_qf = load_w(ws0, "wqf", [T['ws0_free']])
                    q2caps = []
                    for c in range(DCH):
                        ps4 = P4SLOT[c % 2]
                        for d in range(DCH):
                            waits = []
                            if d == 0:
                                waits = [xn1_done, lw_qf]
                                if len(q2caps) >= 2:
                                    waits.append(q2caps[-2])
                                else:
                                    waits += [T['slot'][c % 2], T['psO2_free']]
                            em('t', waits, lambda t, c=c, d=d, ps4=ps4: t.matmul(
                                ps4, ws0[:, d * 1024 + c * 128: d * 1024 + (c + 1) * 128],
                                poolA[:, 8192 + d * 256: 8192 + (d + 1) * 256],
                                start=(d == 0), stop=(d == DCH - 1),
                                skip_group_check=True),
                                inc=I(smm) if d == DCH - 1 else None)
                        mmq = W(smm)
                        em('v', [mmq], lambda v, c=c, ps4=ps4: v.tensor_copy(
                            poolA[:, 10240 + c * 256: 10240 + (c + 1) * 256], ps4),
                            inc=I(sve))
                        q2caps.append(W(sve))
                        T['slot'][c % 2] = q2caps[-1]
                        if c % 2 == 1:
                            yield
                    q2_done = q2caps[-1]
                    if DBG and b == 0 and qt == 0 and qh == 0:
                        ddump("q2", poolA[:, 10240:12288], q2_done)
                    T['ws0_free'] = W(smm)
                    sp_caps = []
                    for rc in range(2):
                        ps4 = P4SLOT[rc % 2]
                        for d in range(DCH):
                            waits = []
                            if d == 0:
                                waits = [q2_done, ffn_tab, T['slot'][rc % 2]]
                            em('t', waits, lambda t, rc=rc, d=d, ps4=ps4: t.matmul(
                                ps4, tab[:, 2048 + d * 256 + rc * 128:
                                         2048 + d * 256 + rc * 128 + 128],
                                poolA[:, 10240 + d * 256: 10240 + (d + 1) * 256],
                                start=(d == 0), stop=(d == DCH - 1),
                                skip_group_check=True),
                                inc=I(smm) if d == DCH - 1 else None)
                        mmz = W(smm)
                        e1 = poolA32[:, 7424:7680]
                        for sgn, off in ((-1.0, 0), (1.0, 2)):
                            em('s', [mmz, W(sex)], lambda s, ps4=ps4, sgn=sgn, e1=e1:
                               s.activation(e1, ps4, AF.Exp, scale=sgn), inc=I(sex))
                            ex1 = W(sex)
                            em('s', [ex1], lambda s, rc=rc, off=off, e1=e1: s.activation(
                                poolA32[:, 6144 + (off + rc) * 256:
                                        6144 + (off + rc + 1) * 256],
                                e1, AF.Ln, bias=onef[:, 0:1]), inc=I(sex))
                        sp_caps.append(W(sex))
                        T['slot'][rc % 2] = sp_caps[-1]
                        yield
                    sp_done = sp_caps[-1]
                    T['q2_free'] = W(smm)
                    yield
                    lp_caps = []
                    for lc in range(2):
                        ps4 = P4SLOT[lc % 2]
                        for kk in range(4):
                            waits = []
                            if kk == 0:
                                waits = [sp_done, const_x, T['slot'][lc % 2]]
                            em('t', waits, lambda t, lc=lc, kk=kk, ps4=ps4: t.matmul(
                                ps4, bmf[:, kk * 256 + lc * 128: kk * 256 + lc * 128 + 128],
                                poolA32[:, 6144 + kk * 256: 6144 + (kk + 1) * 256],
                                start=(kk == 0), stop=(kk == 3),
                                skip_group_check=True),
                                inc=I(smm) if kk == 3 else None)
                        mml = W(smm)
                        em('s', [mml], lambda s, lc=lc, ps4=ps4: s.activation(
                            poolA[:, 14336 + lc * 256: 14336 + (lc + 1) * 256],
                            ps4, AF.Exp), inc=I(sex))
                        lp_caps.append(W(sex))
                        T['slot'][lc % 2] = lp_caps[-1]
                    lp_done = lp_caps[-1]
                    if DBG and b == 0 and qt == 0 and qh == 0:
                        ddump("lg", poolA32[:, 6144:7168], sp_done)
                        ddump("lp", poolA[:, 14336:14848], lp_done)
                    yield
                    yield
                    lw_wout = load_w(ws0, "wout", [T['ws0_free']])
                    retcaps = []
                    for c in range(DCH):
                        ps4 = P4SLOT[c % 2]
                        for lc in range(2):
                            waits = []
                            if lc == 0:
                                waits = [lp_done, T['q2_free']]
                                waits.append(retcaps[-2] if len(retcaps) >= 2
                                             else T['slot'][c % 2])
                            em('t', waits, lambda t, c=c, lc=lc, ps4=ps4: t.matmul(
                                ps4, tab[:, lc * 1024 + c * 128: lc * 1024 + (c + 1) * 128],
                                poolA[:, 14336 + lc * 256: 14336 + (lc + 1) * 256],
                                start=(lc == 0), stop=(lc == 1),
                                skip_group_check=True),
                                inc=I(smm) if lc == 1 else None)
                        mmr = W(smm)
                        em('v', [mmr], lambda v, c=c, ps4=ps4: v.tensor_copy(
                            poolA[:, 10240 + c * 256: 10240 + (c + 1) * 256], ps4),
                            inc=I(sve))
                        retcaps.append(W(sve))
                        T['slot'][c % 2] = retcaps[-1]
                        if c % 4 == 3:
                            yield
                    ret_done = retcaps[-1]
                    if DBG and b == 0 and qt == 0 and qh == 0:
                        ddump("rt", poolA[:, 10240:12288], ret_done)
                    o2caps = []
                    for c in range(DCH):
                        ps4 = P4SLOT[c % 2]
                        for d in range(DCH):
                            waits = []
                            if d == 0:
                                waits = [ret_done, lw_wout]
                                waits.append(o2caps[-2][1] if len(o2caps) >= 2
                                             else T['slot'][c % 2])
                            em('t', waits, lambda t, c=c, d=d, ps4=ps4: t.matmul(
                                ps4, ws0[:, d * 1024 + c * 128: d * 1024 + (c + 1) * 128],
                                poolA[:, 10240 + d * 256: 10240 + (d + 1) * 256],
                                start=(d == 0), stop=(d == DCH - 1),
                                skip_group_check=True),
                                inc=I(smm) if d == DCH - 1 else None)
                        mmo = W(smm)
                        em('v', [mmo, T['aob_free'] if c == 0 else (smm.h, 0)],
                           lambda v, c=c, ps4=ps4: v.tensor_scalar(
                               aob(c), ps4, wbF[:, c:c + 1], None, ALU.add),
                           inc=I(sve))
                        em('v', [T['sqb_free'] if c == 0 else (smm.h, 0)],
                           lambda v, c=c: v.tensor_tensor(
                               sqb[:, c * 256:(c + 1) * 256], aob(c), aob(c),
                               ALU.mult), inc=I(sve))
                        o2caps.append((W(smm), W(sve)))
                        T['slot'][c % 2] = o2caps[-1][1]
                        if c % 2 == 1:
                            yield
                    if DBG and b == 0 and qt == 0 and qh == 0:
                        ddump("o2", poolA32[:, 0:2048], o2caps[-1][1])
                    last_out2_mm = W(smm)
                    T['ws0_free'] = last_out2_mm
                    yield
                    yield
                    yield
                    stats4 = stat_mms(
                        lambda c: aob(c),
                        lambda c: sqb[:, c * 256:(c + 1) * 256],
                        lambda c: [o2caps[-1][1], T['psO2_free']] if c == 0 else [],
                        None, ps_m=psE2[0:1, 256:512], ps_q=psO2[0:1, 256:512])
                    T['sqb_free'] = stats4
                    T['aob_free'] = stats4
                    prev_sqb_free = stats4
                    psum_free, rows4, rst4, nm4 = rows_stats(
                        psE2[0:1, 256:512], psO2[0:1, 256:512], [stats4])
                    T['psO2_free'] = psum_free
                    bb4 = ebounce_rows(rows4, rst4, nm4, T['rsnm_free'])
                    yield
                    st_caps = []
                    for c in range(DCH):
                        x2sl = 7680 + (c % 2) * 256
                        em('v', [bb4] if c == 0 else [],
                           lambda v, c=c: v.tensor_tensor(
                               ttmp[:, 0:256], aob(c),
                               psE2[:, 256:512], ALU.mult), inc=I(sve))
                        em('v', [], lambda v: v.tensor_tensor(
                            ttmp[:, 256:512], ttmp[:, 0:256], psO2[:, 256:512],
                            ALU.add), inc=I(sve))
                        em('v', [st_caps[c - 2]] if c >= 2 else [],
                           lambda v, c=c, x2sl=x2sl: v.tensor_tensor(
                               poolA32[:, x2sl:x2sl + 256], ttmp[:, 256:512],
                               poolA32[:, 2048 + c * 256: 2048 + (c + 1) * 256],
                               ALU.add), inc=I(sve))
                        x2done = W(sve)
                        if c % 3 == 2:
                            yield
                        em('y', [x2done], lambda y, c=c, x2sl=x2sl, qt=qt, h0=h0, b=b:
                           y.dma_start(out=oT[b, c * 128:(c + 1) * 128,
                                             qt * 512 + h0: qt * 512 + h0 + 256],
                                       in_=poolA32[:, x2sl:x2sl + 256]),
                           inc=I(sstp[c % 2], 16))
                        st_caps.append(W(sstp[c % 2]))
                        T['x2r_free'][c % 2] = W(sstp[c % 2])
                    T['rsnm_free'] = W(sve)
                    T['psO2_free'] = W(sve)
                    yield

            _done = object()

            def pump(n=1):
                for _ in range(n):
                    while pend:
                        if next(pend[0], _done) is _done:
                            pend.pop(0)
                            continue
                        break
                    if not pend:
                        return

            for qt in range(NQT):
                gqt = b * NQT + qt
                while pend and T['op_flag'].get(gqt - 2, True) is False:
                    pump(1)
                mid = middle_gen(qt)
                for si, _ in enumerate(mid):
                    if si >= 12:
                        pump(2 if len(pend) > 1 else 1)
                T['op_flag'][gqt] = False
                pend.append(epilogue_gen(qt))
            while pend:
                for _ in pend.pop(0):
                    pass

        fin0, fin1 = W(sst), W(sst1)

        def _final(y, fin0=fin0, fin1=fin1):
            y.wait_ge(fin0[0], fin0[1])
            y.wait_ge(fin1[0], fin1[1])
        P['y'].append(_final)

        with nc.Block() as block:
            @block.tensor
            def _(t):
                for f in P['t']:
                    f(t)

            @block.vector
            def _(v):
                for f in P['v']:
                    f(v)

            @block.scalar
            def _(s):
                for f in P['s']:
                    f(s)

            @block.gpsimd
            def _(g):
                for f in P['g']:
                    f(g)

            @block.sync
            def _(y):
                for f in P['y']:
                    f(y)
    return nc


def _host_inputs(L0_attn_x, L0_attn_Wq, L0_attn_bq, L0_attn_Wk, L0_attn_bk,
                 L0_attn_Wv, L0_attn_bv, L0_attn_Wo, L0_attn_bo, L0_ffn_Wq,
                 L0_ffn_routers, L0_ffn_leaf_keys, L0_ffn_leaf_values,
                 L0_ffn_Wout_w, L0_ffn_Wout_b):
    f = np.float32
    conT = lambda a: np.ascontiguousarray(np.asarray(a), dtype=f)
    wq = conT(np.asarray(L0_attn_Wq).T)
    wk = conT(np.asarray(L0_attn_Wk).T)
    wv = conT(np.asarray(L0_attn_Wv).T)
    wo = conT(np.asarray(L0_attn_Wo).T)
    wqf = conT(np.asarray(L0_ffn_Wq).T)
    wout = conT(np.asarray(L0_ffn_Wout_w).T)
    routersT = np.zeros((E, 256), f)
    routersT[:, 0:255] = np.asarray(L0_ffn_routers).T
    lvh = conT(L0_ffn_leaf_values)
    bias8 = lambda v: conT(np.asarray(v).reshape(8, 128).T)
    bq, bk, bo, wb = (bias8(L0_attn_bq), bias8(L0_attn_bk),
                      bias8(L0_attn_bo), bias8(L0_ffn_Wout_b))
    bv = conT(np.asarray(L0_attn_bv).reshape(1, E))
    invf = 1.0 / 10000.0 ** (np.arange(0, 64, 2, dtype=np.float64) / 64.0)
    j = np.arange(64)
    freq_j = invf[j // 2]
    t = np.arange(L, dtype=np.float64)
    ang = freq_j[:, None] * t[None, :]
    cos64 = np.cos(ang)
    sin64 = np.sin(ang)
    sgn = np.where(j % 2 == 0, -1.0, 1.0)[:, None]
    cosT = conT(np.tile(cos64, (2, 1)))
    sinT = conT(np.tile(sin64 * sgn, (2, 1)))
    pswap = np.zeros((128, 128), f)
    idx = np.arange(128)
    pswap[idx, idx ^ 1] = 1.0
    bmat = np.zeros((512, 256), f)
    for lev in range(8):
        n = 1 << lev
        for m in range(256):
            i = m >> (8 - lev)
            c = (m >> (7 - lev)) & 1
            node = n - 1 + i
            if c:
                bmat[node, m] = -1.0
            else:
                bmat[256 + node, m] = -1.0
    x = np.asarray(L0_attn_x, dtype=f)
    shards = []
    for core in range(8):
        xs = np.ascontiguousarray(
            np.transpose(x[2 * core:2 * core + 2], (0, 2, 1)), dtype=f)
        shards.append({
            "xT": xs, "wq": wq, "wk": wk, "wv": wv, "wo": wo, "wqf": wqf,
            "wout": wout, "routersT": routersT, "lv": lvh, "bmat": bmat,
            "bq": bq, "bk": bk, "bo": bo, "wb": wb, "bv": bv,
            "cosT": cosT, "sinT": sinT, "pswap": pswap,
        })
    return shards


def kernel(**inputs):
    shards = _host_inputs(**inputs)
    nc = build()
    res = run_bass_kernel_spmd(nc, shards, core_ids=list(range(8)))
    out = np.empty((16, 2048, 1024), np.float32)
    for core in range(8):
        o = res.results[core]["oT"]
        out[2 * core] = o[0].T
        out[2 * core + 1] = o[1].T
    return out


# version 65
# speedup vs baseline: 1.2902x; 1.2902x over previous
import numpy as np
import concourse.bass as bass
import concourse.mybir as mybir
from concourse.bass_utils import run_bass_kernel_spmd
from contextlib import ExitStack

F32 = mybir.dt.float32
BF16 = mybir.dt.bfloat16
AF = mybir.ActivationFunctionType
ALU = mybir.AluOpType

NB = 2
E = 1024
L = 2048
DCH = 8
NQT = 4
QT = 512
NQH = 2
QH = 256
KC = 16
NPAIR = 8
EPS = 1e-5


class Sem:
    __slots__ = ("h", "n")


DBG = False


def build():
    nc = bass.Bass("TRN2", target_bir_lowering=False)
    dram_in = lambda n, sh: nc.dram_tensor(n, sh, F32, kind="ExternalInput").ap()
    xT = dram_in("xT", [NB, E, L])
    wTs = {n: dram_in(n, [E, E]) for n in ("wq", "wk", "wv", "wo", "wqf", "wout")}
    routersT = dram_in("routersT", [E, 256])
    lv = dram_in("lv", [256, E])
    bmat = dram_in("bmat", [512, 256])
    bqD = dram_in("bq", [128, 8])
    bkD = dram_in("bk", [128, 8])
    boD = dram_in("bo", [128, 8])
    wbD = dram_in("wb", [128, 8])
    bvD = dram_in("bv", [1, E])
    cosD = dram_in("cosT", [128, L])
    sinD = dram_in("sinT", [128, L])
    pswD = dram_in("pswap", [128, 128])
    oT = nc.dram_tensor("oT", [NB, E, L], F32, kind="ExternalOutput").ap()
    dbg = {}
    if DBG:
        for n, sh in (("A", [128, 4096]), ("vp", [128, 16640]), ("qs", [128, 16384]),
                      ("ks", [128, 16384]), ("xn", [128, 16384]), ("ao", [128, 4096]),
                      ("x1", [128, 2048]), ("r2a", [1, 256]), ("r2b", [1, 256]), ("xn1", [128, 2048]), ("q2", [128, 2048]),
                      ("lg", [128, 1024]), ("lp", [128, 512]), ("rt", [128, 2048]),
                      ("o2", [128, 2048])):
            dbg[n] = nc.dram_tensor("dbg_" + n, sh, F32, kind="ExternalOutput").ap()
    scr = nc.dram_tensor("scr", [512, 512], F32, kind="Internal").ap()
    wstage = nc.dram_tensor("wstage", [6, 128, 8192], BF16, kind="Internal").ap()
    WIDX = {"wq": 0, "wk": 1, "wv": 2, "wo": 3, "wqf": 4, "wout": 5}

    ctx = ExitStack()
    with ctx:
        sb = lambda n, sh, dt: ctx.enter_context(nc.sbuf_tensor(n, sh, dt))
        q_sb = sb("q_sb", [128, 16384], BF16)
        k_sb = sb("k_sb", [128, 16384], BF16)
        vp = sb("vp", [128, 16640], BF16)
        ws0 = sb("ws0", [128, 8192], BF16)
        ws1 = sb("ws1", [128, 8192], BF16)
        poolA = sb("poolA", [128, 16384], BF16)
        poolB = sb("poolB", [128, 8192], BF16)
        expb = sb("expb", [128, 2048], BF16)
        sqb = sb("sqb", [128, 2048], BF16)
        rw = sb("rw", [128, 1024], F32)
        rsnm = sb("rsnm", [128, 512], F32)
        recBb = sb("recBb", [128, 512], BF16)
        ttmp = sb("ttmp", [128, 512], F32)
        tab = sb("tab", [128, 4096], BF16)
        bmf = sb("bmf", [128, 1024], F32)
        pswb = sb("pswb", [128, 128], BF16)
        onesb = sb("onesb", [128, 1], BF16)
        onesrow = sb("onesrow", [1, 128], BF16)
        epsc = sb("epsc", [128, 1], F32)
        onef = sb("onef", [128, 1], F32)
        onesrow_f = sb("onesrow_f", [1, 128], F32)
        invE = sb("invE", [128, 1], BF16)
        invEf = sb("invEf", [128, 1], F32)
        bqF = sb("bqF", [128, 8], F32)
        bkF = sb("bkF", [128, 8], F32)
        boF = sb("boF", [128, 8], F32)
        wbF = sb("wbF", [128, 8], F32)

        psA = ctx.enter_context(nc.psum_tensor("psA", [128, 1024], F32))
        psB = ctx.enter_context(nc.psum_tensor("psB", [128, 1024], F32))
        psE = ctx.enter_context(nc.psum_tensor("psE", [128, 512], F32))
        psO = ctx.enter_context(nc.psum_tensor("psO", [128, 512], F32))
        psE2 = ctx.enter_context(nc.psum_tensor("psE2", [128, 512], F32))
        psO2 = ctx.enter_context(nc.psum_tensor("psO2", [128, 512], F32))

        poolA32 = poolA[:].bitcast(F32)
        poolB32 = poolB[:].bitcast(F32)
        ttmpb = ttmp[:].bitcast(BF16)

        sems = {}
        for n in ("x", "w", "g", "mm", "ex", "ve", "bw", "bb", "st", "mbw", "mbb",
                  "xr0", "xr1", "wl", "tab", "bv", "st1", "gc"):
            s = Sem()
            s.h = ctx.enter_context(nc.semaphore(n))
            s.n = 0
            sems[n] = s
        (sx, sw, sg, smm, sex, sve, sbw, sbb, sst, smbw, smbb, sxr0, sxr1,
         swl, stab, sbv, sst1, sgc) = (
            sems[n] for n in ("x", "w", "g", "mm", "ex", "ve", "bw", "bb", "st",
                              "mbw", "mbb", "xr0", "xr1", "wl", "tab", "bv", "st1",
                              "gc"))
        sxr = (sxr0, sxr1)
        sstp = (sst, sst1)

        P = {k: [] for k in "tvsgy"}

        import traceback as _tb

        def em(eng, waits, buildf, inc=None, incby=None):
            src = _tb.extract_stack()[-2]

            def f(e, waits=tuple(waits), buildf=buildf, inc=inc, src=src):
                try:
                    for (h, v) in waits:
                        if v > 0:
                            e.wait_ge(h, v)
                    ins = buildf(e)
                    if inc is not None:
                        ins.then_inc(inc[0], inc[1])
                except Exception:
                    print("EMIT-FAIL at kernel.py line", src.lineno, ":", src.line)
                    raise
            P[eng].append(f)

        def W(s):
            return (s.h, s.n)

        def I(s, k=1):
            s.n += k
            return (s.h, k)

        P4SLOT = (psE2[:, 0:256], psO2[:, 0:256])
        EST_M = None

        em('g', [], lambda g: g.memset(
            vp[:, 0:16512].rearrange("p (k h c) -> p k h c", k=16, h=8)[:, :, :, 64:65],
            1.0), inc=I(sg, 16))
        em('g', [], lambda g: g.memset(onesb[:], 1.0), inc=I(sg, 16))
        em('g', [], lambda g: g.memset(onesrow[:], 1.0), inc=I(sg, 16))
        em('g', [], lambda g: g.memset(epsc[:], EPS), inc=I(sg, 16))
        em('g', [], lambda g: g.memset(onef[:], 1.0), inc=I(sg, 16))
        em('g', [], lambda g: g.memset(onesrow_f[:], 1.0), inc=I(sg, 16))
        em('g', [], lambda g: g.memset(invE[:], 1.0 / E), inc=I(sg, 16))
        em('g', [], lambda g: g.memset(invEf[:], 1.0 / E), inc=I(sg, 16))
        g_init = W(sg)
        em('g', [], lambda g: g.dma_start(out=pswb[:], in_=pswD[:]), inc=I(sw, 16))
        em('y', [], lambda y: y.dma_start(
            out=bmf[:].rearrange("p (k m) -> p k m", k=4),
            in_=bmat.rearrange("(k p) m -> p k m", p=128)), inc=I(sx, 16))
        for t_sb, t_d in ((bqF, bqD), (bkF, bkD), (boF, boD), (wbF, wbD)):
            em('y', [], lambda y, a=t_sb, b=t_d: y.dma_start(out=a[:], in_=b[:]),
               inc=I(sx, 16))
        const_x = W(sx)
        for i, name in enumerate(("wq", "wk", "wv", "wo", "wqf", "wout")):
            em('g', [W(sbw)], lambda g, n=name: g.dma_start(
                out=ws0[:].rearrange("p (d e) -> p d e", d=8),
                in_=wTs[n].rearrange("(d p) e -> p d e", p=128)),
                inc=I(sw, 16))
            em('y', [W(sw)], lambda y, i=i: y.dma_start(
                out=wstage[i], in_=ws0[:]), inc=I(sbw, 16))
        stage_done = W(sbw)

        def load_w(slot, name, waits):
            em('y', [stage_done] + waits, lambda y, s=slot, i=WIDX[name]:
               y.dma_start(out=s[:], in_=wstage[i]), inc=I(swl, 16))
            return W(swl)

        scr_next = [0]

        def scr_slot():
            s = scr_next[0]
            scr_next[0] += 1
            assert s < 510
            return scr[s:s + 1, :]

        def rows_stats_gen(ps_mean, ps_msq, waits_mm, width=256):
            if width == 256:
                s_mean, s_rst = rw[0:1, 0:256], rw[0:1, 256:512]
                s_scr, s_nm = rw[0:1, 512:768], rw[0:1, 768:1024]
            else:
                s_mean, s_scr = rw[0:1, 0:512], rw[0:1, 512:1024]
                s_rst, s_nm = ttmp[0:1, 0:512], rw[0:1, 512:1024]
            em('v', waits_mm, lambda v: v.tensor_copy(s_mean, ps_mean), inc=I(sve))
            em('v', [], lambda v: v.tensor_tensor(s_scr, s_mean, s_mean, ALU.mult),
               inc=I(sve))
            em('v', [], lambda v: v.tensor_tensor(s_rst, ps_msq, s_scr, ALU.subtract),
               inc=I(sve))
            psum_free = W(sve)
            var_done = W(sve)
            for _ in range(6):
                yield None
            em('s', [var_done, W(sex)], lambda s: s.activation(
                s_scr, s_rst, AF.Ln, bias=epsc[0:1, 0:1]), inc=I(sex))
            ln_done = W(sex)
            yield None
            yield None
            em('s', [ln_done], lambda s: s.activation(s_rst, s_scr, AF.Exp, scale=-0.5),
               inc=I(sex))
            rstd_done = W(sex)
            em('v', [rstd_done], lambda v: v.scalar_tensor_tensor(
                s_nm, s_mean, -1.0, s_rst, ALU.mult, ALU.mult), inc=I(sve))
            return psum_free, W(sve), s_rst, s_nm

        def rows_stats(ps_mean, ps_msq, waits_mm, width=256):
            g = rows_stats_gen(ps_mean, ps_msq, waits_mm, width)
            while True:
                try:
                    next(g)
                except StopIteration as e:
                    return e.value

        def bcast_rows(rows_done, s_rst, s_nm, width=256, dst=None):
            if dst is None:
                dst = (psE[:, 0:256], psE[:, 256:512])
            if not isinstance(rows_done, list):
                rows_done = [rows_done]
            em('t', rows_done, lambda t: t.matmul(
                dst[0], onesrow_f[0:1, 0:128], s_rst,
                start=True, stop=True, skip_group_check=True), inc=None)
            em('t', [], lambda t: t.matmul(
                dst[1], onesrow_f[0:1, 0:128], s_nm,
                start=True, stop=True, skip_group_check=True), inc=I(smm))
            return W(smm)

        def ebounce_rows(rows_done, s_rst, s_nm, rsnm_free):
            em('t', [rows_done], lambda t, s_rst=s_rst: t.matmul(
                psE2[:, 256:512], onesrow_f[0:1, 0:128], s_rst,
                start=True, stop=True, skip_group_check=True), inc=None)
            em('t', [], lambda t, s_nm=s_nm: t.matmul(
                psO2[:, 256:512], onesrow_f[0:1, 0:128], s_nm,
                start=True, stop=True, skip_group_check=True), inc=I(smm))
            return W(smm)

        def stat_mms(src_x, src_sq, first_waits, sq_waits, ps_m=None, ps_q=None,
                     inv=None):
            if ps_m is None:
                ps_m, ps_q = psA[0:1, 0:256], psA[0:1, 512:768]
            if inv is None:
                inv = invE
            for c in range(DCH):
                w = list(first_waits(c)) if first_waits else []
                em('t', w, lambda t, c=c: t.matmul(
                    ps_m, inv[:, 0:1], src_x(c),
                    start=(c == 0), stop=(c == DCH - 1), skip_group_check=True),
                    inc=None)
            for c in range(DCH):
                w = list(sq_waits(c)) if sq_waits else []
                em('t', w, lambda t, c=c: t.matmul(
                    ps_q, invE[:, 0:1], src_sq(c),
                    start=(c == 0), stop=(c == DCH - 1), skip_group_check=True),
                    inc=I(smm) if c == DCH - 1 else None)
            return W(smm)

        pend = []
        T = None

        prev_psA_rows = W(sve)
        prev_sqb_free = W(smm)
        prev_expb_free = W(smm)
        last_out2_mm = W(smm)
        last_op_mm = W(smm)
        T = {
            'exp_set': [W(sex), W(sex)],
            'v_half': [W(smm), W(smm)],
            'norm': W(sve),
            'recB_free': W(sve),
            'rw64_free': W(sve),
            'rw32_free': W(sex),
            'mid_bw': W(smbw),
            'A_free': {0: W(smm), 1: W(smm)},
            'A_ready': {},
            'rsnm_free': W(sve),
            'psO2_free': W(sve),
            'slot': [W(sve), W(sve)],
            'sqb_free': prev_sqb_free,
            'aob_free': W(smm),
            'ws0_free': last_out2_mm,
            'q2_free': W(smm),
            'x2r_free': [W(sst), W(sst1)],
            'op_flag': {},
            'ao_evac': W(sve),
        }

        for b in range(NB):
            lw_q = load_w(ws0, "wq", [last_out2_mm])
            lw_k = load_w(ws1, "wk", [last_op_mm])

            xn_caps = []
            sq_caps = []
            for qt5 in range(4):
                t0 = qt5 * 512
                scr5 = vp[:, qt5 * 4096:(qt5 + 1) * 4096]
                xsrc = xT[b].rearrange("(d p) t -> p d t", p=128)[:, :, t0:t0 + 512]
                semx = sw if qt5 % 2 == 0 else sx
                if qt5 == 0:
                    cw = [T['v_half'][0], T['v_half'][1]]
                elif qt5 >= 2:
                    cw = [sq_caps[qt5 - 2]]
                else:
                    cw = []
                em('g', cw, lambda g, scr5=scr5, xsrc=xsrc: g.dma_start(
                    out=scr5.rearrange("p (d t) -> p d t", d=8), in_=xsrc),
                    inc=I(semx, 16))
                x_ready = W(semx)
                em('v', [x_ready, prev_sqb_free, prev_expb_free],
                   lambda v, scr5=scr5: v.tensor_tensor(
                       sqb[:, 0:2048], scr5[:, 0:2048], scr5[:, 0:2048],
                       ALU.mult), inc=I(sve))
                em('v', [], lambda v, scr5=scr5: v.tensor_tensor(
                    expb[:, 0:2048], scr5[:, 2048:4096], scr5[:, 2048:4096],
                    ALU.mult), inc=I(sve))
                sq_ready = W(sve)
                sq_caps.append(sq_ready)
                for d in range(DCH):
                    w = ([x_ready, prev_psA_rows, g_init,
                          T['exp_set'][0], T['exp_set'][1]] if d == 0 else [])
                    em('t', w, lambda t, d=d, scr5=scr5: t.matmul(
                        psA[0:1, 0:512], invE[:, 0:1],
                        scr5[:, d * 512:(d + 1) * 512],
                        start=(d == 0), stop=(d == DCH - 1),
                        skip_group_check=True), inc=None)
                for d in range(DCH):
                    src = (sqb if d < 4 else expb)
                    off = (d % 4) * 512
                    w = [sq_ready] if d == 0 else []
                    em('t', w, lambda t, d=d, src=src, off=off: t.matmul(
                        psA[0:1, 512:1024], invE[:, 0:1], src[:, off:off + 512],
                        start=(d == 0), stop=(d == DCH - 1),
                        skip_group_check=True),
                        inc=I(smm) if d == DCH - 1 else None)
                stats = W(smm)
                prev_sqb_free = stats
                prev_expb_free = stats
                psum_free, rows_done, rst_r, nm_r = rows_stats(
                    psA[0:1, 0:512], psA[0:1, 512:1024], [stats], width=512)
                prev_psA_rows = psum_free
                bw = [rows_done] if qt5 == 0 else [rows_done, xn_caps[qt5 - 1]]
                bbc = bcast_rows(bw, rst_r, nm_r, width=512,
                                 dst=(psE[:, 0:512], psO[:, 0:512]))
                for d in range(DCH):
                    w = []
                    if d == 0:
                        w = [bbc]
                        if qt5 == 0:
                            w += [T['x2r_free'][0], T['x2r_free'][1]]
                    em('v', w, lambda v, d=d, scr5=scr5: v.tensor_tensor(
                        ttmp[:, 0:512], scr5[:, d * 512:(d + 1) * 512],
                        psE[:, 0:512], ALU.mult))
                    em('v', [], lambda v, d=d, t0=t0: v.tensor_tensor(
                        poolA[:, d * 2048 + t0: d * 2048 + t0 + 512],
                        ttmp[:, 0:512], psO[:, 0:512], ALU.add), inc=I(sve))
                xn_caps.append(W(sve))
            xn_done = xn_caps[-1]
            em('g', [W(smm), W(sve)], lambda g: g.dma_start(
                out=tab[:, 0:2048], in_=cosD[:]), inc=I(stab, 16))
            em('g', [], lambda g: g.dma_start(
                out=tab[:, 2048:4096], in_=sinD[:]), inc=I(stab, 16))
            tab_ready = W(stab)
            em('g', [xn_done], lambda g: g.memset(
                vp[:, 0:16512].rearrange(
                    "p (k h c) -> p k h c", k=16, h=8)[:, :, :, 64:65],
                1.0), inc=I(sg, 16))
            ones_ready = W(sg)

            def qkv_qk(dst, wslot, lw, biasF):
                idx = [(c, qt) for qt in range(NQT) for c in range(DCH)]
                evac_done = {}
                swap_done = {}
                rope_done = {}
                caps = []

                def emit_swap_rope(j):
                    c, qt = idx[j]
                    slot = j % 2
                    psw = psB[:, slot * 512:(slot + 1) * 512]
                    sw_waits = [evac_done[j], tab_ready]
                    if j >= 2:
                        sw_waits.append(rope_done[j - 2])
                    else:
                        sw_waits.append(W(sex))
                    em('t', sw_waits, lambda t, c=c, qt=qt, psw=psw: t.matmul(
                        psw, pswb[:, 0:128],
                        dst[:, c * 2048 + qt * 512: c * 2048 + (qt + 1) * 512],
                        start=True, stop=True, skip_group_check=True), inc=I(smm))
                    swap_done[j] = W(smm)
                    em('v', [swap_done[j]], lambda v, qt=qt, psw=psw: v.tensor_tensor(
                        ttmpb[:, 0:512], psw,
                        tab[:, 2048 + qt * 512: 2048 + (qt + 1) * 512],
                        ALU.mult))
                    em('v', [], lambda v, c=c, qt=qt: v.tensor_tensor(
                        ttmpb[:, 512:1024],
                        dst[:, c * 2048 + qt * 512: c * 2048 + (qt + 1) * 512],
                        tab[:, qt * 512:(qt + 1) * 512], ALU.mult))
                    em('v', [], lambda v, c=c, qt=qt: v.tensor_tensor(
                        dst[:, c * 2048 + qt * 512: c * 2048 + (qt + 1) * 512],
                        ttmpb[:, 0:512], ttmpb[:, 512:1024], ALU.add), inc=I(sve))
                    rope_done[j] = W(sve)

                for i, (c, qt) in enumerate(idx):
                    slot = i % 2
                    ps = psA[:, slot * 512:(slot + 1) * 512]
                    for d in range(DCH):
                        waits = []
                        if d == 0:
                            waits = [lw, xn_caps[qt]]
                            if i >= 2:
                                waits.append(evac_done[i - 2])
                        em('t', waits, lambda t, c=c, qt=qt, d=d, ps=ps: t.matmul(
                            ps, wslot[:, d * 1024 + c * 128: d * 1024 + (c + 1) * 128],
                            poolA[:, d * 2048 + qt * 512: d * 2048 + (qt + 1) * 512],
                            start=(d == 0), stop=(d == DCH - 1),
                            skip_group_check=True),
                            inc=I(smm) if d == DCH - 1 else None)
                    mm_done = W(smm)
                    em('v', [mm_done], lambda v, c=c, qt=qt, ps=ps: v.tensor_scalar(
                        dst[:, c * 2048 + qt * 512: c * 2048 + (qt + 1) * 512],
                        ps, biasF[:, c:c + 1], None, ALU.add), inc=I(sve))
                    evac_done[i] = W(sve)
                    if i > 0:
                        emit_swap_rope(i - 1)
                emit_swap_rope(len(idx) - 1)
                pos = {cq: j for j, cq in enumerate(idx)}
                for c in range(DCH):
                    for qt in range(NQT):
                        j = pos[(c, qt)]
                        caps.append((swap_done[j], rope_done[j]))
                return caps

            q_rope = qkv_qk(q_sb, ws0, lw_q, bqF)
            lw_v = load_w(ws0, "wv", [q_rope[-1][0]])
            k_rope = qkv_qk(k_sb, ws1, lw_k, bkF)
            lw_o = load_w(ws1, "wo", [k_rope[-1][0]])

            rwb = rw[:].bitcast(BF16)
            em('g', [W(sve), W(smm)], lambda g, rwb=rwb: g.dma_start(
                out=rwb[0:1, 0:512], in_=bvD[0:1, 0:512]), inc=I(sbv, 16))
            em('g', [], lambda g, rwb=rwb: g.dma_start(
                out=rwb[0:1, 512:1024], in_=bvD[0:1, 512:1024]), inc=I(sbv, 16))
            bv_ready = W(sbv)
            v_evac = []
            for kt in range(KC):
                for eh in range(2):
                    ps = (psE if (kt * 2 + eh) % 2 == 0 else psO)[:, 0:512]
                    for d in range(DCH):
                        waits = []
                        if d == 0:
                            waits = [lw_v, xn_done, g_init]
                            if len(v_evac) >= 2:
                                waits.append(v_evac[-2])
                        em('t', waits, lambda t, kt=kt, d=d, eh=eh, ps=ps: t.matmul(
                            ps, poolA[:, d * 2048 + kt * 128: d * 2048 + (kt + 1) * 128],
                            ws0[:, d * 1024 + eh * 512: d * 1024 + (eh + 1) * 512],
                            start=(d == 0), stop=False, skip_group_check=True),
                            inc=None)
                    em('t', [bv_ready] if len(v_evac) == 0 else [],
                       lambda t, eh=eh, ps=ps: t.matmul(
                           ps, onesrow[0:1, 0:128],
                           rw[:].bitcast(BF16)[0:1, eh * 512:(eh + 1) * 512],
                           start=False, stop=True, skip_group_check=True), inc=I(smm))
                    vmm = W(smm)
                    base = lambda kt, eh: kt * 1032 + eh * 4 * 129
                    em('v', [vmm, ones_ready] if len(v_evac) == 0 else [vmm],
                       lambda v, kt=kt, eh=eh, ps=ps, base=base:
                       v.tensor_copy(
                           vp[:, base(kt, eh): base(kt, eh) + 4 * 129]
                           .rearrange("p (pr c) -> p pr c", pr=4)[:, :, 0:64],
                           ps.rearrange("p (h c) -> p h c", h=8)[:, 0::2, :]),
                       inc=I(sve))
                    em('v', [], lambda v, kt=kt, eh=eh, ps=ps, base=base:
                       v.tensor_copy(
                           vp[:, base(kt, eh) + 65: base(kt, eh) + 65 + 4 * 129]
                           .rearrange("p (pr c) -> p pr c", pr=4)[:, :, 0:64],
                           ps.rearrange("p (h c) -> p h c", h=8)[:, 1::2, :]),
                       inc=I(sve))
                    v_evac.append(W(sve))
            v_done = v_evac[-1]
            last_out2_mm = W(smm)

            em('g', [q_rope[-1][0], k_rope[-1][0], W(sve)],
               lambda g: g.dma_start(
                   out=tab[:, 0:2048].rearrange("p (l e) -> p l e", l=2),
                   in_=lv.rearrange("(l p) e -> p l e", p=128)), inc=I(stab, 16))
            em('g', [], lambda g: g.dma_start(
                out=tab[:, 2048:4096].rearrange("p (d r) -> p d r", d=8),
                in_=routersT.rearrange("(d p) r -> p d r", p=128)), inc=I(stab, 16))
            ffn_tab = W(stab)

            def Abuf(qt):
                p = qt % 2
                return poolB[:, p * 4096:(p + 1) * 4096]

            def middle_gen(qt):
                steps = [(c, kc) for c in range(NPAIR) for kc in range(KC)]
                scores_done = {}
                mlazy = []

                def mpump(n):
                    for _ in range(n):
                        while mlazy:
                            try:
                                next(mlazy[0])
                                break
                            except StopIteration:
                                mlazy.pop(0)
                        if not mlazy:
                            return

                def emit_scores(i):
                    c, kc = steps[i]
                    sset = kc % 2
                    sc = psA if sset == 0 else psB
                    waits = [T['exp_set'][sset]]
                    if kc == 0:
                        waits.append(q_rope[c * NQT + qt][1])
                        if c == 0 and qt == 0:
                            waits += [k_rope[-1][1], v_done]
                    em('t', waits, lambda t, c=c, kc=kc, qt=qt, sc=sc: t.matmul(
                        sc[:, 0:512],
                        k_sb[0:64, c * 2048 + kc * 128: c * 2048 + (kc + 1) * 128],
                        q_sb[0:64, c * 2048 + qt * 512: c * 2048 + (qt + 1) * 512],
                        start=True, stop=True, skip_group_check=True), inc=None)
                    em('t', [], lambda t, c=c, kc=kc, qt=qt, sc=sc: t.matmul(
                        sc[:, 512:1024],
                        k_sb[64:128, c * 2048 + kc * 128: c * 2048 + (kc + 1) * 128],
                        q_sb[64:128, c * 2048 + qt * 512: c * 2048 + (qt + 1) * 512],
                        start=True, stop=True, skip_group_check=True), inc=I(smm))
                    scores_done[i] = W(smm)

                emit_scores(0)
                for i, (c, kc) in enumerate(steps):
                    sset = kc % 2
                    sc = psA if sset == 0 else psB
                    half = kc % 2
                    em('s', [scores_done[i], T['v_half'][half]],
                       lambda s, sc=sc, half=half: s.activation(
                           expb[:, half * 1024:(half + 1) * 1024], sc[:, 0:1024],
                           AF.Exp, scale=0.125), inc=I(sex))
                    T['exp_set'][sset] = W(sex)
                    exp_done = T['exp_set'][sset]
                    if i + 1 < len(steps):
                        emit_scores(i + 1)
                    vw = [exp_done]
                    if kc == 0:
                        vw.append(T['norm'])
                    em('t', vw, lambda t, c=c, kc=kc, half=half: t.matmul(
                        psE[0:65, :],
                        vp[:, kc * 1032 + c * 129: kc * 1032 + c * 129 + 65],
                        expb[:, half * 1024: half * 1024 + 512],
                        start=(kc == 0), stop=(kc == KC - 1),
                        skip_group_check=True), inc=None)
                    em('t', [], lambda t, c=c, kc=kc, half=half: t.matmul(
                        psO[64:128, :],
                        vp[:, kc * 1032 + c * 129 + 65: kc * 1032 + c * 129 + 129],
                        expb[:, half * 1024 + 512: half * 1024 + 1024],
                        start=(kc == 0), stop=(kc == KC - 1),
                        tile_position=(0, 64), skip_group_check=True), inc=None)
                    em('t', [], lambda t, kc=kc, half=half: t.matmul(
                        psO[32:33, :], onesb[:, 0:1],
                        expb[:, half * 1024 + 512: half * 1024 + 1024],
                        start=(kc == 0), stop=(kc == KC - 1),
                        tile_position=(0, 32), skip_group_check=True), inc=I(smm))
                    T['v_half'][half] = W(smm)
                    if kc == KC - 1:
                        while mlazy:
                            mpump(1)
                        pair_mm = W(smm)
                        em('s', [pair_mm, T['rw64_free']], lambda s: s.copy(
                            out=rw[64:65, 512:1024], in_=psE[64:65, :]), inc=I(sex))
                        em('s', [T['rw32_free']], lambda s: s.copy(
                            out=rw[32:33, 512:1024], in_=psO[32:33, :]), inc=I(sex))
                        rows_cap = W(sex)
                        em('s', [T['A_free'][qt % 2]] if c < 2 else [],
                           lambda s, c=c, qt=qt: s.copy(
                               out=Abuf(qt)[0:64, c * 512:(c + 1) * 512],
                               in_=psE[0:64, :]), inc=I(sex))
                        em('s', [], lambda s, c=c, qt=qt: s.copy(
                            out=Abuf(qt)[64:128, c * 512:(c + 1) * 512],
                            in_=psO[64:128, :]), inc=I(sex))
                        T['norm'] = W(sex)

                        def lazy_gen(c=c, qt=qt, rows_cap=rows_cap,
                                     acap=T['norm']):
                            sl_e, sl_o = scr_slot(), scr_slot()
                            em('g', [rows_cap], lambda g, sl_e=sl_e: g.dma_start(
                                out=sl_e[:, 0:512], in_=rw[64:65, 512:1024]),
                                inc=I(smbw, 16))
                            em('g', [], lambda g, sl_o=sl_o: g.dma_start(
                                out=sl_o[:, 0:512], in_=rw[32:33, 512:1024]),
                                inc=I(smbw, 16))
                            T['mid_bw'] = W(smbw)
                            T['rw64_free'] = T['mid_bw']
                            T['rw32_free'] = T['mid_bw']
                            yield
                            em('g', [T['mid_bw'], T['recB_free']],
                               lambda g, sl_e=sl_e: g.dma_start(
                                   out=recBb[0:64, 0:512],
                                   in_=sl_e[:, 0:512].to_broadcast((64, 512))),
                               inc=I(smbb, 16))
                            em('g', [], lambda g, sl_o=sl_o: g.dma_start(
                                out=recBb[64:128, 0:512],
                                in_=sl_o[:, 0:512].to_broadcast((64, 512))),
                                inc=I(smbb, 16))
                            bbr = W(smbb)
                            yield
                            def _recip(v):
                                with nc.allow_low_precision(
                                        reason="bf16 denom bcast (matches prior "
                                               "bf16-bounced reciprocal rows)"):
                                    return v.reciprocal(
                                        recBb[:, 0:512], recBb[:, 0:512])
                            em('v', [bbr], _recip, inc=I(sve))
                            yield
                            em('v', [acap], lambda v, c=c, qt=qt: v.tensor_tensor(
                                Abuf(qt)[0:64, c * 512:(c + 1) * 512],
                                Abuf(qt)[0:64, c * 512:(c + 1) * 512],
                                recBb[0:64, 0:512], ALU.mult))
                            em('v', [], lambda v, c=c, qt=qt: v.tensor_tensor(
                                Abuf(qt)[64:128, c * 512:(c + 1) * 512],
                                Abuf(qt)[64:128, c * 512:(c + 1) * 512],
                                recBb[64:128, 0:512], ALU.mult), inc=I(sve))
                            T['recB_free'] = W(sve)
                            T['A_ready'][qt] = W(sve)

                        mlazy.append(lazy_gen())
                    mpump(2)
                    yield
                while mlazy:
                    mpump(1)

            def ddump(name, src_ap, wait):
                if DBG:
                    em('g', [wait], lambda g, s_=src_ap, d=dbg[name]: g.dma_start(
                        out=d[:], in_=s_), inc=I(sw, 16))

            def epilogue_gen(qt):
                nonlocal prev_sqb_free, last_out2_mm, last_op_mm
                AB = Abuf(qt)
                aob = lambda c: poolA[:, 12288 + c * 256: 12288 + (c + 1) * 256]
                if DBG and b == 0 and qt == 0:
                    ddump("A", AB, T['A_ready'][qt])
                    ddump("vp", vp[:], T['A_ready'][qt])
                    ddump("qs", q_sb[:], T['A_ready'][qt])
                    ddump("ks", k_sb[:], T['A_ready'][qt])
                    ddump("xn", poolA[:], T['A_ready'][qt])
                for qh in range(NQH):
                    h0 = qh * 256
                    oev = []
                    if True:
                        for c in range(DCH):
                            ps4 = P4SLOT[c % 2]
                            for j in range(DCH):
                                waits = []
                                if j == 0:
                                    waits = [T['A_ready'][qt], lw_o]
                                    if len(oev) >= 2:
                                        waits.append(oev[-2][1])
                                    else:
                                        waits += [T['slot'][c % 2], T['psO2_free']]
                                em('t', waits, lambda t, c=c, j=j, ps4=ps4, h0=h0:
                                   t.matmul(
                                       ps4,
                                       ws1[:, j * 1024 + c * 128: j * 1024 + (c + 1) * 128],
                                       AB[:, j * 512 + h0: j * 512 + h0 + 256],
                                       start=(j == 0), stop=(j == DCH - 1),
                                       skip_group_check=True),
                                   inc=I(smm) if j == DCH - 1 else None)
                            mmc = W(smm)
                            em('v', [mmc, T['aob_free'] if c == 0 else (smm.h, 0)],
                               lambda v, c=c, ps4=ps4, h0=h0: v.tensor_scalar(
                                   poolA[:, c * 512 + h0: c * 512 + h0 + 256], ps4,
                                   boF[:, c:c + 1], None, ALU.add), inc=I(sve))
                            oev.append((W(smm), W(sve)))
                            T['slot'][c % 2] = oev[-1][1]
                            yield
                        T['ao_evac'] = W(sve)
                    aoh = lambda c, h0=h0: poolA[:, c * 512 + h0: c * 512 + h0 + 256]
                    for c in range(DCH):
                        em('v', [T['ao_evac'], T['sqb_free']] if c == 0 else [],
                           lambda v, c=c, aoh=aoh: v.tensor_tensor(
                               sqb[:, c * 256:(c + 1) * 256], aoh(c), aoh(c),
                               ALU.mult), inc=I(sve))
                        if c == 3:
                            yield
                    sq_done = W(sve)
                    if qh == NQH - 1:
                        T['A_free'][qt % 2] = W(smm)
                        T['op_flag'][b * NQT + qt] = True
                    last_op_mm = W(smm)
                    yield
                    yield
                    yield
                    if qh == 0:
                        yield
                        yield
                        yield
                    stats2 = stat_mms(
                        lambda c, aoh=aoh: aoh(c),
                        lambda c: sqb[:, c * 256:(c + 1) * 256],
                        lambda c: [sq_done, T['psO2_free']] if c == 0 else [],
                        None, ps_m=psE2[0:1, 256:512], ps_q=psO2[0:1, 256:512])
                    T['sqb_free'] = stats2
                    T['aob_free'] = stats2
                    prev_sqb_free = stats2
                    psum_free, rows2, rst2, nm2 = rows_stats(
                        psE2[0:1, 256:512], psO2[0:1, 256:512], [stats2])
                    T['psO2_free'] = psum_free
                    bb2 = ebounce_rows(rows2, rst2, nm2, T['rsnm_free'])
                    yield
                    if DBG and b == 0 and qt == 0 and qh == 0:
                        ddump("ao", poolA[:, 0:4096], T['ao_evac'])
                    x1caps = []
                    for c in range(DCH):
                        xrsl = 7680 + (c % 2) * 256
                        em('y', [bb2, T['x2r_free'][c % 2]] if c < 2
                           else [x1caps[c - 2]],
                           lambda y, c=c, xrsl=xrsl, h0=h0, qt=qt, b=b: y.dma_start(
                               out=poolA32[:, xrsl:xrsl + 256],
                               in_=xT[b, c * 128:(c + 1) * 128,
                                      qt * 512 + h0: qt * 512 + h0 + 256]),
                           inc=I(sxr[c % 2], 16))
                        xr_ready = W(sxr[c % 2])
                        em('v', [bb2, xr_ready], lambda v, c=c, aoh=aoh: v.tensor_tensor(
                            ttmp[:, 0:256], aoh(c),
                            psE2[:, 256:512], ALU.mult))
                        em('v', [], lambda v: v.tensor_tensor(
                            ttmp[:, 256:512], ttmp[:, 0:256], psO2[:, 256:512],
                            ALU.add))
                        em('v', [], lambda v, c=c, xrsl=xrsl: v.tensor_tensor(
                            poolA32[:, 2048 + c * 256: 2048 + (c + 1) * 256],
                            ttmp[:, 256:512], poolA32[:, xrsl:xrsl + 256], ALU.add),
                            inc=I(sve))
                        x1caps.append(W(sve))
                        if c % 3 == 2:
                            yield
                    T['rsnm_free'] = W(sve)
                    yield
                    if DBG and b == 0 and qt == 0 and qh == 0:
                        ddump("x1", poolA32[:, 2048:4096], x1caps[-1])
                        ddump("r2a", rw[0:1, 256:512], rows2)
                        ddump("r2b", rw[0:1, 768:1024], rows2)
                    for c in range(DCH):
                        em('v', [T['sqb_free']] if c == 0 else [],
                           lambda v, c=c: v.tensor_tensor(
                               sqb[:, c * 256:(c + 1) * 256],
                               poolA32[:, 2048 + c * 256: 2048 + (c + 1) * 256],
                               poolA32[:, 2048 + c * 256: 2048 + (c + 1) * 256],
                               ALU.mult), inc=I(sve))
                    ln3_src = W(sve)
                    yield
                    yield
                    yield
                    if qh == 0:
                        yield
                        yield
                    stats3 = stat_mms(
                        lambda c: poolA32[:, 2048 + c * 256: 2048 + (c + 1) * 256],
                        lambda c: sqb[:, c * 256:(c + 1) * 256],
                        lambda c: [ln3_src, T['psO2_free']] if c == 0 else [],
                        None, ps_m=psE2[0:1, 256:512], ps_q=psO2[0:1, 256:512],
                        inv=invEf)
                    T['sqb_free'] = stats3
                    prev_sqb_free = stats3
                    psum_free, rows3, rst3, nm3 = rows_stats(
                        psE2[0:1, 256:512], psO2[0:1, 256:512], [stats3])
                    T['psO2_free'] = psum_free
                    bb3 = ebounce_rows(rows3, rst3, nm3, T['rsnm_free'])
                    yield
                    for c in range(DCH):
                        em('v', [bb3] if c == 0 else [], lambda v, c=c: v.tensor_tensor(
                            ttmp[:, 0:256],
                            poolA32[:, 2048 + c * 256: 2048 + (c + 1) * 256],
                            psE2[:, 256:512], ALU.mult))
                        em('v', [], lambda v, c=c: v.tensor_tensor(
                            poolA[:, 8192 + c * 256: 8192 + (c + 1) * 256],
                            ttmp[:, 0:256], psO2[:, 256:512], ALU.add), inc=I(sve))
                        if c % 3 == 2:
                            yield
                    xn1_done = W(sve)
                    T['rsnm_free'] = W(sve)
                    T['psO2_free'] = W(sve)
                    yield
                    if DBG and b == 0 and qt == 0 and qh == 0:
                        ddump("xn1", poolA[:, 8192:10240], xn1_done)
                    yield
                    yield
                    yield
                    lw_qf = load_w(ws0, "wqf", [T['ws0_free']])
                    q2caps = []
                    for c in range(DCH):
                        ps4 = P4SLOT[c % 2]
                        for d in range(DCH):
                            waits = []
                            if d == 0:
                                waits = [xn1_done, lw_qf]
                                if len(q2caps) >= 2:
                                    waits.append(q2caps[-2])
                                else:
                                    waits += [T['slot'][c % 2], T['psO2_free']]
                            em('t', waits, lambda t, c=c, d=d, ps4=ps4: t.matmul(
                                ps4, ws0[:, d * 1024 + c * 128: d * 1024 + (c + 1) * 128],
                                poolA[:, 8192 + d * 256: 8192 + (d + 1) * 256],
                                start=(d == 0), stop=(d == DCH - 1),
                                skip_group_check=True),
                                inc=I(smm) if d == DCH - 1 else None)
                        mmq = W(smm)
                        em('v', [mmq], lambda v, c=c, ps4=ps4: v.tensor_copy(
                            poolA[:, 10240 + c * 256: 10240 + (c + 1) * 256], ps4),
                            inc=I(sve))
                        q2caps.append(W(sve))
                        T['slot'][c % 2] = q2caps[-1]
                        if c % 2 == 1:
                            yield
                    q2_done = q2caps[-1]
                    if DBG and b == 0 and qt == 0 and qh == 0:
                        ddump("q2", poolA[:, 10240:12288], q2_done)
                    T['ws0_free'] = W(smm)
                    sp_caps = []
                    for rc in range(2):
                        ps4 = P4SLOT[rc % 2]
                        for d in range(DCH):
                            waits = []
                            if d == 0:
                                waits = [q2_done, ffn_tab, T['slot'][rc % 2]]
                            em('t', waits, lambda t, rc=rc, d=d, ps4=ps4: t.matmul(
                                ps4, tab[:, 2048 + d * 256 + rc * 128:
                                         2048 + d * 256 + rc * 128 + 128],
                                poolA[:, 10240 + d * 256: 10240 + (d + 1) * 256],
                                start=(d == 0), stop=(d == DCH - 1),
                                skip_group_check=True),
                                inc=I(smm) if d == DCH - 1 else None)
                        mmz = W(smm)
                        e1 = poolA32[:, 7424:7680]
                        for sgn, off in ((-1.0, 0), (1.0, 2)):
                            em('s', [mmz, W(sex)], lambda s, ps4=ps4, sgn=sgn, e1=e1:
                               s.activation(e1, ps4, AF.Exp, scale=sgn), inc=I(sex))
                            ex1 = W(sex)
                            em('s', [ex1], lambda s, rc=rc, off=off, e1=e1: s.activation(
                                poolA32[:, 6144 + (off + rc) * 256:
                                        6144 + (off + rc + 1) * 256],
                                e1, AF.Ln, bias=onef[:, 0:1]), inc=I(sex))
                        sp_caps.append(W(sex))
                        T['slot'][rc % 2] = sp_caps[-1]
                        yield
                    sp_done = sp_caps[-1]
                    T['q2_free'] = W(smm)
                    yield
                    lp_caps = []
                    for lc in range(2):
                        ps4 = P4SLOT[lc % 2]
                        for kk in range(4):
                            waits = []
                            if kk == 0:
                                waits = [sp_done, const_x, T['slot'][lc % 2]]
                            em('t', waits, lambda t, lc=lc, kk=kk, ps4=ps4: t.matmul(
                                ps4, bmf[:, kk * 256 + lc * 128: kk * 256 + lc * 128 + 128],
                                poolA32[:, 6144 + kk * 256: 6144 + (kk + 1) * 256],
                                start=(kk == 0), stop=(kk == 3),
                                skip_group_check=True),
                                inc=I(smm) if kk == 3 else None)
                        mml = W(smm)
                        em('s', [mml], lambda s, lc=lc, ps4=ps4: s.activation(
                            poolA[:, 14336 + lc * 256: 14336 + (lc + 1) * 256],
                            ps4, AF.Exp), inc=I(sex))
                        lp_caps.append(W(sex))
                        T['slot'][lc % 2] = lp_caps[-1]
                    lp_done = lp_caps[-1]
                    if DBG and b == 0 and qt == 0 and qh == 0:
                        ddump("lg", poolA32[:, 6144:7168], sp_done)
                        ddump("lp", poolA[:, 14336:14848], lp_done)
                    yield
                    yield
                    lw_wout = load_w(ws0, "wout", [T['ws0_free']])
                    retcaps = []
                    for c in range(DCH):
                        ps4 = P4SLOT[c % 2]
                        for lc in range(2):
                            waits = []
                            if lc == 0:
                                waits = [lp_done, T['q2_free']]
                                waits.append(retcaps[-2] if len(retcaps) >= 2
                                             else T['slot'][c % 2])
                            em('t', waits, lambda t, c=c, lc=lc, ps4=ps4: t.matmul(
                                ps4, tab[:, lc * 1024 + c * 128: lc * 1024 + (c + 1) * 128],
                                poolA[:, 14336 + lc * 256: 14336 + (lc + 1) * 256],
                                start=(lc == 0), stop=(lc == 1),
                                skip_group_check=True),
                                inc=I(smm) if lc == 1 else None)
                        mmr = W(smm)
                        em('v', [mmr], lambda v, c=c, ps4=ps4: v.tensor_copy(
                            poolA[:, 10240 + c * 256: 10240 + (c + 1) * 256], ps4),
                            inc=I(sve))
                        retcaps.append(W(sve))
                        T['slot'][c % 2] = retcaps[-1]
                        if c % 4 == 3:
                            yield
                    ret_done = retcaps[-1]
                    if DBG and b == 0 and qt == 0 and qh == 0:
                        ddump("rt", poolA[:, 10240:12288], ret_done)
                    o2caps = []
                    for c in range(DCH):
                        ps4 = P4SLOT[c % 2]
                        for d in range(DCH):
                            waits = []
                            if d == 0:
                                waits = [ret_done, lw_wout]
                                waits.append(o2caps[-2][1] if len(o2caps) >= 2
                                             else T['slot'][c % 2])
                            em('t', waits, lambda t, c=c, d=d, ps4=ps4: t.matmul(
                                ps4, ws0[:, d * 1024 + c * 128: d * 1024 + (c + 1) * 128],
                                poolA[:, 10240 + d * 256: 10240 + (d + 1) * 256],
                                start=(d == 0), stop=(d == DCH - 1),
                                skip_group_check=True),
                                inc=I(smm) if d == DCH - 1 else None)
                        mmo = W(smm)
                        em('v', [mmo, T['aob_free'] if c == 0 else (smm.h, 0)],
                           lambda v, c=c, ps4=ps4: v.tensor_scalar(
                               aob(c), ps4, wbF[:, c:c + 1], None, ALU.add),
                           inc=I(sve))
                        em('v', [T['sqb_free'] if c == 0 else (smm.h, 0)],
                           lambda v, c=c: v.tensor_tensor(
                               sqb[:, c * 256:(c + 1) * 256], aob(c), aob(c),
                               ALU.mult), inc=I(sve))
                        o2caps.append((W(smm), W(sve)))
                        T['slot'][c % 2] = o2caps[-1][1]
                        if c % 2 == 1:
                            yield
                    if DBG and b == 0 and qt == 0 and qh == 0:
                        ddump("o2", poolA32[:, 0:2048], o2caps[-1][1])
                    last_out2_mm = W(smm)
                    T['ws0_free'] = last_out2_mm
                    yield
                    yield
                    yield
                    stats4 = stat_mms(
                        lambda c: aob(c),
                        lambda c: sqb[:, c * 256:(c + 1) * 256],
                        lambda c: [o2caps[-1][1], T['psO2_free']] if c == 0 else [],
                        None, ps_m=psE2[0:1, 256:512], ps_q=psO2[0:1, 256:512])
                    T['sqb_free'] = stats4
                    T['aob_free'] = stats4
                    prev_sqb_free = stats4
                    psum_free, rows4, rst4, nm4 = rows_stats(
                        psE2[0:1, 256:512], psO2[0:1, 256:512], [stats4])
                    T['psO2_free'] = psum_free
                    bb4 = ebounce_rows(rows4, rst4, nm4, T['rsnm_free'])
                    yield
                    st_caps = []
                    for c in range(DCH):
                        x2sl = 7680 + (c % 2) * 256
                        em('v', [bb4] if c == 0 else [],
                           lambda v, c=c: v.tensor_tensor(
                               ttmp[:, 0:256], aob(c),
                               psE2[:, 256:512], ALU.mult))
                        em('v', [], lambda v: v.tensor_tensor(
                            ttmp[:, 256:512], ttmp[:, 0:256], psO2[:, 256:512],
                            ALU.add))
                        em('v', [st_caps[c - 2]] if c >= 2 else [],
                           lambda v, c=c, x2sl=x2sl: v.tensor_tensor(
                               poolA32[:, x2sl:x2sl + 256], ttmp[:, 256:512],
                               poolA32[:, 2048 + c * 256: 2048 + (c + 1) * 256],
                               ALU.add), inc=I(sve))
                        x2done = W(sve)
                        if c % 3 == 2:
                            yield
                        em('y', [x2done], lambda y, c=c, x2sl=x2sl, qt=qt, h0=h0, b=b:
                           y.dma_start(out=oT[b, c * 128:(c + 1) * 128,
                                             qt * 512 + h0: qt * 512 + h0 + 256],
                                       in_=poolA32[:, x2sl:x2sl + 256]),
                           inc=I(sstp[c % 2], 16))
                        st_caps.append(W(sstp[c % 2]))
                        T['x2r_free'][c % 2] = W(sstp[c % 2])
                    T['rsnm_free'] = W(sve)
                    T['psO2_free'] = W(sve)
                    yield

            _done = object()

            def pump(n=1):
                for _ in range(n):
                    while pend:
                        if next(pend[0], _done) is _done:
                            pend.pop(0)
                            continue
                        break
                    if not pend:
                        return

            for qt in range(NQT):
                gqt = b * NQT + qt
                while pend and T['op_flag'].get(gqt - 2, True) is False:
                    pump(1)
                mid = middle_gen(qt)
                for si, _ in enumerate(mid):
                    if si >= 12:
                        pump(2 if len(pend) > 1 else 1)
                T['op_flag'][gqt] = False
                pend.append(epilogue_gen(qt))
            while pend:
                for _ in pend.pop(0):
                    pass

        fin0, fin1 = W(sst), W(sst1)

        def _final(y, fin0=fin0, fin1=fin1):
            y.wait_ge(fin0[0], fin0[1])
            y.wait_ge(fin1[0], fin1[1])
        P['y'].append(_final)

        with nc.Block() as block:
            @block.tensor
            def _(t):
                for f in P['t']:
                    f(t)

            @block.vector
            def _(v):
                for f in P['v']:
                    f(v)

            @block.scalar
            def _(s):
                for f in P['s']:
                    f(s)

            @block.gpsimd
            def _(g):
                for f in P['g']:
                    f(g)

            @block.sync
            def _(y):
                for f in P['y']:
                    f(y)
    return nc


def _host_inputs(L0_attn_x, L0_attn_Wq, L0_attn_bq, L0_attn_Wk, L0_attn_bk,
                 L0_attn_Wv, L0_attn_bv, L0_attn_Wo, L0_attn_bo, L0_ffn_Wq,
                 L0_ffn_routers, L0_ffn_leaf_keys, L0_ffn_leaf_values,
                 L0_ffn_Wout_w, L0_ffn_Wout_b):
    f = np.float32
    conT = lambda a: np.ascontiguousarray(np.asarray(a), dtype=f)
    wq = conT(np.asarray(L0_attn_Wq).T)
    wk = conT(np.asarray(L0_attn_Wk).T)
    wv = conT(np.asarray(L0_attn_Wv).T)
    wo = conT(np.asarray(L0_attn_Wo).T)
    wqf = conT(np.asarray(L0_ffn_Wq).T)
    wout = conT(np.asarray(L0_ffn_Wout_w).T)
    routersT = np.zeros((E, 256), f)
    routersT[:, 0:255] = np.asarray(L0_ffn_routers).T
    lvh = conT(L0_ffn_leaf_values)
    bias8 = lambda v: conT(np.asarray(v).reshape(8, 128).T)
    bq, bk, bo, wb = (bias8(L0_attn_bq), bias8(L0_attn_bk),
                      bias8(L0_attn_bo), bias8(L0_ffn_Wout_b))
    bv = conT(np.asarray(L0_attn_bv).reshape(1, E))
    invf = 1.0 / 10000.0 ** (np.arange(0, 64, 2, dtype=np.float64) / 64.0)
    j = np.arange(64)
    freq_j = invf[j // 2]
    t = np.arange(L, dtype=np.float64)
    ang = freq_j[:, None] * t[None, :]
    cos64 = np.cos(ang)
    sin64 = np.sin(ang)
    sgn = np.where(j % 2 == 0, -1.0, 1.0)[:, None]
    cosT = conT(np.tile(cos64, (2, 1)))
    sinT = conT(np.tile(sin64 * sgn, (2, 1)))
    pswap = np.zeros((128, 128), f)
    idx = np.arange(128)
    pswap[idx, idx ^ 1] = 1.0
    bmat = np.zeros((512, 256), f)
    for lev in range(8):
        n = 1 << lev
        for m in range(256):
            i = m >> (8 - lev)
            c = (m >> (7 - lev)) & 1
            node = n - 1 + i
            if c:
                bmat[node, m] = -1.0
            else:
                bmat[256 + node, m] = -1.0
    x = np.asarray(L0_attn_x, dtype=f)
    shards = []
    for core in range(8):
        xs = np.ascontiguousarray(
            np.transpose(x[2 * core:2 * core + 2], (0, 2, 1)), dtype=f)
        shards.append({
            "xT": xs, "wq": wq, "wk": wk, "wv": wv, "wo": wo, "wqf": wqf,
            "wout": wout, "routersT": routersT, "lv": lvh, "bmat": bmat,
            "bq": bq, "bk": bk, "bo": bo, "wb": wb, "bv": bv,
            "cosT": cosT, "sinT": sinT, "pswap": pswap,
        })
    return shards


def kernel(**inputs):
    shards = _host_inputs(**inputs)
    nc = build()
    res = run_bass_kernel_spmd(nc, shards, core_ids=list(range(8)))
    out = np.empty((16, 2048, 1024), np.float32)
    for core in range(8):
        o = res.results[core]["oT"]
        out[2 * core] = o[0].T
        out[2 * core + 1] = o[1].T
    return out


# version 66 (final)
# speedup vs baseline: 1.2956x; 1.0042x over previous
import numpy as np
import concourse.bass as bass
import concourse.mybir as mybir
from concourse.bass_utils import run_bass_kernel_spmd
from contextlib import ExitStack

F32 = mybir.dt.float32
BF16 = mybir.dt.bfloat16
AF = mybir.ActivationFunctionType
ALU = mybir.AluOpType

NB = 2
E = 1024
L = 2048
DCH = 8
NQT = 4
QT = 512
NQH = 2
QH = 256
KC = 16
NPAIR = 8
EPS = 1e-5


class Sem:
    __slots__ = ("h", "n")


DBG = False


def build():
    nc = bass.Bass("TRN2", target_bir_lowering=False)
    dram_in = lambda n, sh: nc.dram_tensor(n, sh, F32, kind="ExternalInput").ap()
    xT = dram_in("xT", [NB, E, L])
    wTs = {n: dram_in(n, [E, E]) for n in ("wq", "wk", "wv", "wo", "wqf", "wout")}
    routersT = dram_in("routersT", [E, 256])
    lv = dram_in("lv", [256, E])
    bmat = dram_in("bmat", [512, 256])
    bqD = dram_in("bq", [128, 8])
    bkD = dram_in("bk", [128, 8])
    boD = dram_in("bo", [128, 8])
    wbD = dram_in("wb", [128, 8])
    bvD = dram_in("bv", [1, E])
    cosD = dram_in("cosT", [128, L])
    sinD = dram_in("sinT", [128, L])
    pswD = dram_in("pswap", [128, 128])
    oT = nc.dram_tensor("oT", [NB, E, L], F32, kind="ExternalOutput").ap()
    dbg = {}
    if DBG:
        for n, sh in (("A", [128, 4096]), ("vp", [128, 16640]), ("qs", [128, 16384]),
                      ("ks", [128, 16384]), ("xn", [128, 16384]), ("ao", [128, 4096]),
                      ("x1", [128, 2048]), ("r2a", [1, 256]), ("r2b", [1, 256]), ("xn1", [128, 2048]), ("q2", [128, 2048]),
                      ("lg", [128, 1024]), ("lp", [128, 512]), ("rt", [128, 2048]),
                      ("o2", [128, 2048])):
            dbg[n] = nc.dram_tensor("dbg_" + n, sh, F32, kind="ExternalOutput").ap()
    scr = nc.dram_tensor("scr", [512, 512], F32, kind="Internal").ap()
    wstage = nc.dram_tensor("wstage", [6, 128, 8192], BF16, kind="Internal").ap()
    WIDX = {"wq": 0, "wk": 1, "wv": 2, "wo": 3, "wqf": 4, "wout": 5}

    ctx = ExitStack()
    with ctx:
        sb = lambda n, sh, dt: ctx.enter_context(nc.sbuf_tensor(n, sh, dt))
        q_sb = sb("q_sb", [128, 16384], BF16)
        k_sb = sb("k_sb", [128, 16384], BF16)
        vp = sb("vp", [128, 16640], BF16)
        ws0 = sb("ws0", [128, 8192], BF16)
        ws1 = sb("ws1", [128, 8192], BF16)
        poolA = sb("poolA", [128, 16384], BF16)
        poolB = sb("poolB", [128, 8192], BF16)
        expb = sb("expb", [128, 2048], BF16)
        sqb = sb("sqb", [128, 2048], BF16)
        rw = sb("rw", [128, 1024], F32)
        rsnm = sb("rsnm", [128, 512], F32)
        recBb = sb("recBb", [128, 512], BF16)
        ttmp = sb("ttmp", [128, 512], F32)
        tab = sb("tab", [128, 4096], BF16)
        bmf = sb("bmf", [128, 1024], F32)
        pswb = sb("pswb", [128, 128], BF16)
        onesb = sb("onesb", [128, 1], BF16)
        onesrow = sb("onesrow", [1, 128], BF16)
        epsc = sb("epsc", [128, 1], F32)
        onef = sb("onef", [128, 1], F32)
        onesrow_f = sb("onesrow_f", [1, 128], F32)
        invE = sb("invE", [128, 1], BF16)
        invEf = sb("invEf", [128, 1], F32)
        bqF = sb("bqF", [128, 8], F32)
        bkF = sb("bkF", [128, 8], F32)
        boF = sb("boF", [128, 8], F32)
        wbF = sb("wbF", [128, 8], F32)

        psA = ctx.enter_context(nc.psum_tensor("psA", [128, 1024], F32))
        psB = ctx.enter_context(nc.psum_tensor("psB", [128, 1024], F32))
        psE = ctx.enter_context(nc.psum_tensor("psE", [128, 512], F32))
        psO = ctx.enter_context(nc.psum_tensor("psO", [128, 512], F32))
        psE2 = ctx.enter_context(nc.psum_tensor("psE2", [128, 512], F32))
        psO2 = ctx.enter_context(nc.psum_tensor("psO2", [128, 512], F32))

        poolA32 = poolA[:].bitcast(F32)
        poolB32 = poolB[:].bitcast(F32)
        ttmpb = ttmp[:].bitcast(BF16)

        sems = {}
        for n in ("x", "w", "g", "mm", "ex", "ve", "bw", "bb", "st", "mbw", "mbb",
                  "xr0", "xr1", "wl", "tab", "bv", "st1", "gc"):
            s = Sem()
            s.h = ctx.enter_context(nc.semaphore(n))
            s.n = 0
            sems[n] = s
        (sx, sw, sg, smm, sex, sve, sbw, sbb, sst, smbw, smbb, sxr0, sxr1,
         swl, stab, sbv, sst1, sgc) = (
            sems[n] for n in ("x", "w", "g", "mm", "ex", "ve", "bw", "bb", "st",
                              "mbw", "mbb", "xr0", "xr1", "wl", "tab", "bv", "st1",
                              "gc"))
        sxr = (sxr0, sxr1)
        sstp = (sst, sst1)

        P = {k: [] for k in "tvsgy"}

        import traceback as _tb

        def em(eng, waits, buildf, inc=None, incby=None):
            src = _tb.extract_stack()[-2]

            def f(e, waits=tuple(waits), buildf=buildf, inc=inc, src=src):
                try:
                    for (h, v) in waits:
                        if v > 0:
                            e.wait_ge(h, v)
                    ins = buildf(e)
                    if inc is not None:
                        ins.then_inc(inc[0], inc[1])
                except Exception:
                    print("EMIT-FAIL at kernel.py line", src.lineno, ":", src.line)
                    raise
            P[eng].append(f)

        def W(s):
            return (s.h, s.n)

        def I(s, k=1):
            s.n += k
            return (s.h, k)

        P4SLOT = (psE2[:, 0:256], psO2[:, 0:256])
        EST_M = None

        em('g', [], lambda g: g.memset(
            vp[:, 0:16512].rearrange("p (k h c) -> p k h c", k=16, h=8)[:, :, :, 64:65],
            1.0), inc=I(sg, 16))
        em('g', [], lambda g: g.memset(onesb[:], 1.0), inc=I(sg, 16))
        em('g', [], lambda g: g.memset(onesrow[:], 1.0), inc=I(sg, 16))
        em('g', [], lambda g: g.memset(epsc[:], EPS), inc=I(sg, 16))
        em('g', [], lambda g: g.memset(onef[:], 1.0), inc=I(sg, 16))
        em('g', [], lambda g: g.memset(onesrow_f[:], 1.0), inc=I(sg, 16))
        em('g', [], lambda g: g.memset(invE[:], 1.0 / E), inc=I(sg, 16))
        em('g', [], lambda g: g.memset(invEf[:], 1.0 / E), inc=I(sg, 16))
        g_init = W(sg)
        em('g', [], lambda g: g.dma_start(out=pswb[:], in_=pswD[:]), inc=I(sw, 16))
        em('y', [], lambda y: y.dma_start(
            out=bmf[:].rearrange("p (k m) -> p k m", k=4),
            in_=bmat.rearrange("(k p) m -> p k m", p=128)), inc=I(sx, 16))
        for t_sb, t_d in ((bqF, bqD), (bkF, bkD), (boF, boD), (wbF, wbD)):
            em('y', [], lambda y, a=t_sb, b=t_d: y.dma_start(out=a[:], in_=b[:]),
               inc=I(sx, 16))
        const_x = W(sx)
        for i, name in enumerate(("wq", "wk", "wv", "wo", "wqf", "wout")):
            em('g', [W(sbw)], lambda g, n=name: g.dma_start(
                out=ws0[:].rearrange("p (d e) -> p d e", d=8),
                in_=wTs[n].rearrange("(d p) e -> p d e", p=128)),
                inc=I(sw, 16))
            em('y', [W(sw)], lambda y, i=i: y.dma_start(
                out=wstage[i], in_=ws0[:]), inc=I(sbw, 16))
        stage_done = W(sbw)

        def load_w(slot, name, waits):
            em('y', [stage_done] + waits, lambda y, s=slot, i=WIDX[name]:
               y.dma_start(out=s[:], in_=wstage[i]), inc=I(swl, 16))
            return W(swl)

        scr_next = [0]

        def scr_slot():
            s = scr_next[0]
            scr_next[0] += 1
            assert s < 510
            return scr[s:s + 1, :]

        def rows_stats_gen(ps_mean, ps_msq, waits_mm, width=256):
            if width == 256:
                s_mean, s_rst = rw[0:1, 0:256], rw[0:1, 256:512]
                s_scr, s_nm = rw[0:1, 512:768], rw[0:1, 768:1024]
            else:
                s_mean, s_scr = rw[0:1, 0:512], rw[0:1, 512:1024]
                s_rst, s_nm = ttmp[0:1, 0:512], rw[0:1, 512:1024]
            em('v', waits_mm, lambda v: v.tensor_copy(s_mean, ps_mean), inc=I(sve))
            em('v', [], lambda v: v.tensor_tensor(s_scr, s_mean, s_mean, ALU.mult),
               inc=I(sve))
            em('v', [], lambda v: v.tensor_tensor(s_rst, ps_msq, s_scr, ALU.subtract),
               inc=I(sve))
            psum_free = W(sve)
            var_done = W(sve)
            for _ in range(6):
                yield None
            em('s', [var_done, W(sex)], lambda s: s.activation(
                s_scr, s_rst, AF.Ln, bias=epsc[0:1, 0:1]), inc=I(sex))
            ln_done = W(sex)
            yield None
            yield None
            em('s', [ln_done], lambda s: s.activation(s_rst, s_scr, AF.Exp, scale=-0.5),
               inc=I(sex))
            rstd_done = W(sex)
            em('v', [rstd_done], lambda v: v.scalar_tensor_tensor(
                s_nm, s_mean, -1.0, s_rst, ALU.mult, ALU.mult), inc=I(sve))
            return psum_free, W(sve), s_rst, s_nm

        def rows_stats(ps_mean, ps_msq, waits_mm, width=256):
            g = rows_stats_gen(ps_mean, ps_msq, waits_mm, width)
            while True:
                try:
                    next(g)
                except StopIteration as e:
                    return e.value

        def bcast_rows(rows_done, s_rst, s_nm, width=256, dst=None):
            if dst is None:
                dst = (psE[:, 0:256], psE[:, 256:512])
            if not isinstance(rows_done, list):
                rows_done = [rows_done]
            em('t', rows_done, lambda t: t.matmul(
                dst[0], onesrow_f[0:1, 0:128], s_rst,
                start=True, stop=True, skip_group_check=True), inc=None)
            em('t', [], lambda t: t.matmul(
                dst[1], onesrow_f[0:1, 0:128], s_nm,
                start=True, stop=True, skip_group_check=True), inc=I(smm))
            return W(smm)

        def ebounce_rows(rows_done, s_rst, s_nm, rsnm_free):
            em('t', [rows_done], lambda t, s_rst=s_rst: t.matmul(
                psE2[:, 256:512], onesrow_f[0:1, 0:128], s_rst,
                start=True, stop=True, skip_group_check=True), inc=None)
            em('t', [], lambda t, s_nm=s_nm: t.matmul(
                psO2[:, 256:512], onesrow_f[0:1, 0:128], s_nm,
                start=True, stop=True, skip_group_check=True), inc=I(smm))
            return W(smm)

        def stat_mms(src_x, src_sq, first_waits, sq_waits, ps_m=None, ps_q=None,
                     inv=None):
            if ps_m is None:
                ps_m, ps_q = psA[0:1, 0:256], psA[0:1, 512:768]
            if inv is None:
                inv = invE
            for c in range(DCH):
                w = list(first_waits(c)) if first_waits else []
                em('t', w, lambda t, c=c: t.matmul(
                    ps_m, inv[:, 0:1], src_x(c),
                    start=(c == 0), stop=(c == DCH - 1), skip_group_check=True),
                    inc=None)
            for c in range(DCH):
                w = list(sq_waits(c)) if sq_waits else []
                em('t', w, lambda t, c=c: t.matmul(
                    ps_q, invE[:, 0:1], src_sq(c),
                    start=(c == 0), stop=(c == DCH - 1), skip_group_check=True),
                    inc=I(smm) if c == DCH - 1 else None)
            return W(smm)

        pend = []
        T = None

        prev_psA_rows = W(sve)
        prev_sqb_free = W(smm)
        prev_expb_free = W(smm)
        last_out2_mm = W(smm)
        last_op_mm = W(smm)
        T = {
            'exp_set': [W(sex), W(sex)],
            'v_half': [W(smm), W(smm)],
            'norm': W(sve),
            'recB_free': W(sve),
            'rw64_free': W(sve),
            'rw32_free': W(sex),
            'mid_bw': W(smbw),
            'A_free': {0: W(smm), 1: W(smm)},
            'A_ready': {},
            'rsnm_free': W(sve),
            'psO2_free': W(sve),
            'slot': [W(sve), W(sve)],
            'sqb_free': prev_sqb_free,
            'aob_free': W(smm),
            'ws0_free': last_out2_mm,
            'q2_free': W(smm),
            'x2r_free': [W(sst), W(sst1)],
            'op_flag': {},
            'ao_evac': W(sve),
        }

        for b in range(NB):
            lw_q = load_w(ws0, "wq", [last_out2_mm])
            lw_k = load_w(ws1, "wk", [last_op_mm])

            xn_caps = []
            sq_caps = []
            for qt5 in range(4):
                t0 = qt5 * 512
                scr5 = vp[:, qt5 * 4096:(qt5 + 1) * 4096]
                xsrc = xT[b].rearrange("(d p) t -> p d t", p=128)[:, :, t0:t0 + 512]
                semx = sw if qt5 % 2 == 0 else sx
                if qt5 == 0:
                    cw = [T['v_half'][0], T['v_half'][1]]
                elif qt5 >= 2:
                    cw = [sq_caps[qt5 - 2]]
                else:
                    cw = []
                em('g', cw, lambda g, scr5=scr5, xsrc=xsrc: g.dma_start(
                    out=scr5.rearrange("p (d t) -> p d t", d=8), in_=xsrc),
                    inc=I(semx, 16))
                x_ready = W(semx)
                em('v', [x_ready, prev_sqb_free, prev_expb_free],
                   lambda v, scr5=scr5: v.tensor_tensor(
                       sqb[:, 0:2048], scr5[:, 0:2048], scr5[:, 0:2048],
                       ALU.mult), inc=I(sve))
                em('v', [], lambda v, scr5=scr5: v.tensor_tensor(
                    expb[:, 0:2048], scr5[:, 2048:4096], scr5[:, 2048:4096],
                    ALU.mult), inc=I(sve))
                sq_ready = W(sve)
                sq_caps.append(sq_ready)
                for d in range(DCH):
                    w = ([x_ready, prev_psA_rows, g_init,
                          T['exp_set'][0], T['exp_set'][1]] if d == 0 else [])
                    em('t', w, lambda t, d=d, scr5=scr5: t.matmul(
                        psA[0:1, 0:512], invE[:, 0:1],
                        scr5[:, d * 512:(d + 1) * 512],
                        start=(d == 0), stop=(d == DCH - 1),
                        skip_group_check=True), inc=None)
                for d in range(DCH):
                    src = (sqb if d < 4 else expb)
                    off = (d % 4) * 512
                    w = [sq_ready] if d == 0 else []
                    em('t', w, lambda t, d=d, src=src, off=off: t.matmul(
                        psA[0:1, 512:1024], invE[:, 0:1], src[:, off:off + 512],
                        start=(d == 0), stop=(d == DCH - 1),
                        skip_group_check=True),
                        inc=I(smm) if d == DCH - 1 else None)
                stats = W(smm)
                prev_sqb_free = stats
                prev_expb_free = stats
                psum_free, rows_done, rst_r, nm_r = rows_stats(
                    psA[0:1, 0:512], psA[0:1, 512:1024], [stats], width=512)
                prev_psA_rows = psum_free
                bw = [rows_done] if qt5 == 0 else [rows_done, xn_caps[qt5 - 1]]
                bbc = bcast_rows(bw, rst_r, nm_r, width=512,
                                 dst=(psE[:, 0:512], psO[:, 0:512]))
                for d in range(DCH):
                    w = []
                    if d == 0:
                        w = [bbc]
                        if qt5 == 0:
                            w += [T['x2r_free'][0], T['x2r_free'][1]]
                    em('v', w, lambda v, d=d, scr5=scr5: v.tensor_tensor(
                        ttmp[:, 0:512], scr5[:, d * 512:(d + 1) * 512],
                        psE[:, 0:512], ALU.mult))
                    em('v', [], lambda v, d=d, t0=t0: v.tensor_tensor(
                        poolA[:, d * 2048 + t0: d * 2048 + t0 + 512],
                        ttmp[:, 0:512], psO[:, 0:512], ALU.add), inc=I(sve))
                xn_caps.append(W(sve))
            xn_done = xn_caps[-1]
            em('g', [W(smm), W(sve)], lambda g: g.dma_start(
                out=tab[:, 0:2048], in_=cosD[:]), inc=I(stab, 16))
            em('g', [], lambda g: g.dma_start(
                out=tab[:, 2048:4096], in_=sinD[:]), inc=I(stab, 16))
            tab_ready = W(stab)
            em('g', [xn_done], lambda g: g.memset(
                vp[:, 0:16512].rearrange(
                    "p (k h c) -> p k h c", k=16, h=8)[:, :, :, 64:65],
                1.0), inc=I(sg, 16))
            ones_ready = W(sg)

            def qkv_qk(dst, wslot, lw, biasF):
                idx = [(c, qt) for qt in range(NQT) for c in range(DCH)]
                evac_done = {}
                swap_done = {}
                rope_done = {}
                caps = []

                def emit_swap_rope(j):
                    c, qt = idx[j]
                    slot = j % 2
                    psw = psB[:, slot * 512:(slot + 1) * 512]
                    sw_waits = [evac_done[j], tab_ready]
                    if j >= 2:
                        sw_waits.append(rope_done[j - 2])
                    else:
                        sw_waits.append(W(sex))
                    em('t', sw_waits, lambda t, c=c, qt=qt, psw=psw: t.matmul(
                        psw, pswb[:, 0:128],
                        dst[:, c * 2048 + qt * 512: c * 2048 + (qt + 1) * 512],
                        start=True, stop=True, skip_group_check=True), inc=I(smm))
                    swap_done[j] = W(smm)
                    em('v', [swap_done[j]], lambda v, qt=qt, psw=psw: v.tensor_tensor(
                        ttmpb[:, 0:512], psw,
                        tab[:, 2048 + qt * 512: 2048 + (qt + 1) * 512],
                        ALU.mult))
                    em('v', [], lambda v, c=c, qt=qt: v.tensor_tensor(
                        ttmpb[:, 512:1024],
                        dst[:, c * 2048 + qt * 512: c * 2048 + (qt + 1) * 512],
                        tab[:, qt * 512:(qt + 1) * 512], ALU.mult))
                    em('v', [], lambda v, c=c, qt=qt: v.tensor_tensor(
                        dst[:, c * 2048 + qt * 512: c * 2048 + (qt + 1) * 512],
                        ttmpb[:, 0:512], ttmpb[:, 512:1024], ALU.add), inc=I(sve))
                    rope_done[j] = W(sve)

                for i, (c, qt) in enumerate(idx):
                    slot = i % 2
                    ps = psA[:, slot * 512:(slot + 1) * 512]
                    for d in range(DCH):
                        waits = []
                        if d == 0:
                            waits = [lw, xn_caps[qt]]
                            if i >= 2:
                                waits.append(evac_done[i - 2])
                        em('t', waits, lambda t, c=c, qt=qt, d=d, ps=ps: t.matmul(
                            ps, wslot[:, d * 1024 + c * 128: d * 1024 + (c + 1) * 128],
                            poolA[:, d * 2048 + qt * 512: d * 2048 + (qt + 1) * 512],
                            start=(d == 0), stop=(d == DCH - 1),
                            skip_group_check=True),
                            inc=I(smm) if d == DCH - 1 else None)
                    mm_done = W(smm)
                    em('v', [mm_done], lambda v, c=c, qt=qt, ps=ps: v.tensor_scalar(
                        dst[:, c * 2048 + qt * 512: c * 2048 + (qt + 1) * 512],
                        ps, biasF[:, c:c + 1], None, ALU.add), inc=I(sve))
                    evac_done[i] = W(sve)
                    if i > 0:
                        emit_swap_rope(i - 1)
                emit_swap_rope(len(idx) - 1)
                pos = {cq: j for j, cq in enumerate(idx)}
                for c in range(DCH):
                    for qt in range(NQT):
                        j = pos[(c, qt)]
                        caps.append((swap_done[j], rope_done[j]))
                return caps

            q_rope = qkv_qk(q_sb, ws0, lw_q, bqF)
            lw_v = load_w(ws0, "wv", [q_rope[-1][0]])
            k_rope = qkv_qk(k_sb, ws1, lw_k, bkF)
            lw_o = load_w(ws1, "wo", [k_rope[-1][0]])

            rwb = rw[:].bitcast(BF16)
            em('g', [W(sve), W(smm)], lambda g, rwb=rwb: g.dma_start(
                out=rwb[0:1, 0:512], in_=bvD[0:1, 0:512]), inc=I(sbv, 16))
            em('g', [], lambda g, rwb=rwb: g.dma_start(
                out=rwb[0:1, 512:1024], in_=bvD[0:1, 512:1024]), inc=I(sbv, 16))
            bv_ready = W(sbv)
            v_evac = []
            for kt in range(KC):
                for eh in range(2):
                    ps = (psE if (kt * 2 + eh) % 2 == 0 else psO)[:, 0:512]
                    for d in range(DCH):
                        waits = []
                        if d == 0:
                            waits = [lw_v, xn_done, g_init]
                            if len(v_evac) >= 2:
                                waits.append(v_evac[-2])
                        em('t', waits, lambda t, kt=kt, d=d, eh=eh, ps=ps: t.matmul(
                            ps, poolA[:, d * 2048 + kt * 128: d * 2048 + (kt + 1) * 128],
                            ws0[:, d * 1024 + eh * 512: d * 1024 + (eh + 1) * 512],
                            start=(d == 0), stop=False, skip_group_check=True),
                            inc=None)
                    em('t', [bv_ready] if len(v_evac) == 0 else [],
                       lambda t, eh=eh, ps=ps: t.matmul(
                           ps, onesrow[0:1, 0:128],
                           rw[:].bitcast(BF16)[0:1, eh * 512:(eh + 1) * 512],
                           start=False, stop=True, skip_group_check=True), inc=I(smm))
                    vmm = W(smm)
                    base = lambda kt, eh: kt * 1032 + eh * 4 * 129
                    em('v', [vmm, ones_ready] if len(v_evac) == 0 else [vmm],
                       lambda v, kt=kt, eh=eh, ps=ps, base=base:
                       v.tensor_copy(
                           vp[:, base(kt, eh): base(kt, eh) + 4 * 129]
                           .rearrange("p (pr c) -> p pr c", pr=4)[:, :, 0:64],
                           ps.rearrange("p (h c) -> p h c", h=8)[:, 0::2, :]),
                       inc=I(sve))
                    em('v', [], lambda v, kt=kt, eh=eh, ps=ps, base=base:
                       v.tensor_copy(
                           vp[:, base(kt, eh) + 65: base(kt, eh) + 65 + 4 * 129]
                           .rearrange("p (pr c) -> p pr c", pr=4)[:, :, 0:64],
                           ps.rearrange("p (h c) -> p h c", h=8)[:, 1::2, :]),
                       inc=I(sve))
                    v_evac.append(W(sve))
            v_done = v_evac[-1]
            last_out2_mm = W(smm)

            em('g', [q_rope[-1][0], k_rope[-1][0], W(sve)],
               lambda g: g.dma_start(
                   out=tab[:, 0:2048].rearrange("p (l e) -> p l e", l=2),
                   in_=lv.rearrange("(l p) e -> p l e", p=128)), inc=I(stab, 16))
            em('g', [], lambda g: g.dma_start(
                out=tab[:, 2048:4096].rearrange("p (d r) -> p d r", d=8),
                in_=routersT.rearrange("(d p) r -> p d r", p=128)), inc=I(stab, 16))
            ffn_tab = W(stab)

            def Abuf(qt):
                p = qt % 2
                return poolB[:, p * 4096:(p + 1) * 4096]

            def middle_gen(qt):
                steps = [(c, kc) for c in range(NPAIR) for kc in range(KC)]
                scores_done = {}
                mlazy = []

                def mpump(n):
                    for _ in range(n):
                        while mlazy:
                            try:
                                next(mlazy[0])
                                break
                            except StopIteration:
                                mlazy.pop(0)
                        if not mlazy:
                            return

                def emit_scores(i):
                    c, kc = steps[i]
                    sset = kc % 2
                    sc = psA if sset == 0 else psB
                    waits = [T['exp_set'][sset]]
                    if kc == 0:
                        waits.append(q_rope[c * NQT + qt][1])
                        if c == 0 and qt == 0:
                            waits += [k_rope[-1][1], v_done]
                    em('t', waits, lambda t, c=c, kc=kc, qt=qt, sc=sc: t.matmul(
                        sc[:, 0:512],
                        k_sb[0:64, c * 2048 + kc * 128: c * 2048 + (kc + 1) * 128],
                        q_sb[0:64, c * 2048 + qt * 512: c * 2048 + (qt + 1) * 512],
                        start=True, stop=True, skip_group_check=True), inc=None)
                    em('t', [], lambda t, c=c, kc=kc, qt=qt, sc=sc: t.matmul(
                        sc[:, 512:1024],
                        k_sb[64:128, c * 2048 + kc * 128: c * 2048 + (kc + 1) * 128],
                        q_sb[64:128, c * 2048 + qt * 512: c * 2048 + (qt + 1) * 512],
                        start=True, stop=True, skip_group_check=True), inc=I(smm))
                    scores_done[i] = W(smm)

                emit_scores(0)
                for i, (c, kc) in enumerate(steps):
                    sset = kc % 2
                    sc = psA if sset == 0 else psB
                    half = kc % 2
                    em('s', [scores_done[i], T['v_half'][half]],
                       lambda s, sc=sc, half=half: s.activation(
                           expb[:, half * 1024:(half + 1) * 1024], sc[:, 0:1024],
                           AF.Exp, scale=0.125), inc=I(sex))
                    T['exp_set'][sset] = W(sex)
                    exp_done = T['exp_set'][sset]
                    if i + 1 < len(steps):
                        emit_scores(i + 1)
                    vw = [exp_done]
                    if kc == 0:
                        vw.append(T['norm'])
                    em('t', vw, lambda t, c=c, kc=kc, half=half: t.matmul(
                        psE[0:65, :],
                        vp[:, kc * 1032 + c * 129: kc * 1032 + c * 129 + 65],
                        expb[:, half * 1024: half * 1024 + 512],
                        start=(kc == 0), stop=(kc == KC - 1),
                        skip_group_check=True), inc=None)
                    em('t', [], lambda t, c=c, kc=kc, half=half: t.matmul(
                        psO[64:128, :],
                        vp[:, kc * 1032 + c * 129 + 65: kc * 1032 + c * 129 + 129],
                        expb[:, half * 1024 + 512: half * 1024 + 1024],
                        start=(kc == 0), stop=(kc == KC - 1),
                        tile_position=(0, 64), skip_group_check=True), inc=None)
                    em('t', [], lambda t, kc=kc, half=half: t.matmul(
                        psO[32:33, :], onesb[:, 0:1],
                        expb[:, half * 1024 + 512: half * 1024 + 1024],
                        start=(kc == 0), stop=(kc == KC - 1),
                        tile_position=(0, 32), skip_group_check=True), inc=I(smm))
                    T['v_half'][half] = W(smm)
                    if kc == KC - 1:
                        while mlazy:
                            mpump(1)
                        pair_mm = W(smm)
                        em('s', [pair_mm, T['rw64_free']], lambda s: s.copy(
                            out=rw[64:65, 512:1024], in_=psE[64:65, :]), inc=I(sex))
                        em('s', [T['rw32_free']], lambda s: s.copy(
                            out=rw[32:33, 512:1024], in_=psO[32:33, :]), inc=I(sex))
                        rows_cap = W(sex)
                        em('s', [T['A_free'][qt % 2]] if c < 2 else [],
                           lambda s, c=c, qt=qt: s.copy(
                               out=Abuf(qt)[0:64, c * 512:(c + 1) * 512],
                               in_=psE[0:64, :]), inc=I(sex))
                        em('s', [], lambda s, c=c, qt=qt: s.copy(
                            out=Abuf(qt)[64:128, c * 512:(c + 1) * 512],
                            in_=psO[64:128, :]), inc=I(sex))
                        T['norm'] = W(sex)

                        def lazy_gen(c=c, qt=qt, rows_cap=rows_cap,
                                     acap=T['norm']):
                            sl_e, sl_o = scr_slot(), scr_slot()
                            em('g', [rows_cap], lambda g, sl_e=sl_e: g.dma_start(
                                out=sl_e[:, 0:512], in_=rw[64:65, 512:1024]),
                                inc=I(smbw, 16))
                            em('g', [], lambda g, sl_o=sl_o: g.dma_start(
                                out=sl_o[:, 0:512], in_=rw[32:33, 512:1024]),
                                inc=I(smbw, 16))
                            T['mid_bw'] = W(smbw)
                            T['rw64_free'] = T['mid_bw']
                            T['rw32_free'] = T['mid_bw']
                            yield
                            em('g', [T['mid_bw'], T['recB_free']],
                               lambda g, sl_e=sl_e: g.dma_start(
                                   out=recBb[0:64, 0:512],
                                   in_=sl_e[:, 0:512].to_broadcast((64, 512))),
                               inc=I(smbb, 16))
                            em('g', [], lambda g, sl_o=sl_o: g.dma_start(
                                out=recBb[64:128, 0:512],
                                in_=sl_o[:, 0:512].to_broadcast((64, 512))),
                                inc=I(smbb, 16))
                            bbr = W(smbb)
                            yield
                            def _recip(v):
                                with nc.allow_low_precision(
                                        reason="bf16 denom bcast (matches prior "
                                               "bf16-bounced reciprocal rows)"):
                                    return v.reciprocal(
                                        recBb[:, 0:512], recBb[:, 0:512])
                            em('v', [bbr], _recip, inc=I(sve))
                            yield
                            em('v', [acap], lambda v, c=c, qt=qt: v.tensor_tensor(
                                Abuf(qt)[0:128, c * 512:(c + 1) * 512],
                                Abuf(qt)[0:128, c * 512:(c + 1) * 512],
                                recBb[0:128, 0:512], ALU.mult), inc=I(sve))
                            T['recB_free'] = W(sve)
                            T['A_ready'][qt] = W(sve)

                        mlazy.append(lazy_gen())
                    mpump(2)
                    yield
                while mlazy:
                    mpump(1)

            def ddump(name, src_ap, wait):
                if DBG:
                    em('g', [wait], lambda g, s_=src_ap, d=dbg[name]: g.dma_start(
                        out=d[:], in_=s_), inc=I(sw, 16))

            def epilogue_gen(qt):
                nonlocal prev_sqb_free, last_out2_mm, last_op_mm
                AB = Abuf(qt)
                aob = lambda c: poolA[:, 12288 + c * 256: 12288 + (c + 1) * 256]
                if DBG and b == 0 and qt == 0:
                    ddump("A", AB, T['A_ready'][qt])
                    ddump("vp", vp[:], T['A_ready'][qt])
                    ddump("qs", q_sb[:], T['A_ready'][qt])
                    ddump("ks", k_sb[:], T['A_ready'][qt])
                    ddump("xn", poolA[:], T['A_ready'][qt])
                for qh in range(NQH):
                    h0 = qh * 256
                    oev = []
                    if True:
                        for c in range(DCH):
                            ps4 = P4SLOT[c % 2]
                            for j in range(DCH):
                                waits = []
                                if j == 0:
                                    waits = [T['A_ready'][qt], lw_o]
                                    if len(oev) >= 2:
                                        waits.append(oev[-2][1])
                                    else:
                                        waits += [T['slot'][c % 2], T['psO2_free']]
                                em('t', waits, lambda t, c=c, j=j, ps4=ps4, h0=h0:
                                   t.matmul(
                                       ps4,
                                       ws1[:, j * 1024 + c * 128: j * 1024 + (c + 1) * 128],
                                       AB[:, j * 512 + h0: j * 512 + h0 + 256],
                                       start=(j == 0), stop=(j == DCH - 1),
                                       skip_group_check=True),
                                   inc=I(smm) if j == DCH - 1 else None)
                            mmc = W(smm)
                            em('v', [mmc, T['aob_free'] if c == 0 else (smm.h, 0)],
                               lambda v, c=c, ps4=ps4, h0=h0: v.tensor_scalar(
                                   poolA[:, c * 512 + h0: c * 512 + h0 + 256], ps4,
                                   boF[:, c:c + 1], None, ALU.add), inc=I(sve))
                            oev.append((W(smm), W(sve)))
                            T['slot'][c % 2] = oev[-1][1]
                            yield
                        T['ao_evac'] = W(sve)
                    aoh = lambda c, h0=h0: poolA[:, c * 512 + h0: c * 512 + h0 + 256]
                    for c in range(DCH):
                        em('v', [T['ao_evac'], T['sqb_free']] if c == 0 else [],
                           lambda v, c=c, aoh=aoh: v.tensor_tensor(
                               sqb[:, c * 256:(c + 1) * 256], aoh(c), aoh(c),
                               ALU.mult), inc=I(sve))
                        if c == 3:
                            yield
                    sq_done = W(sve)
                    if qh == NQH - 1:
                        T['A_free'][qt % 2] = W(smm)
                        T['op_flag'][b * NQT + qt] = True
                    last_op_mm = W(smm)
                    yield
                    yield
                    yield
                    if qh == 0:
                        yield
                        yield
                        yield
                    stats2 = stat_mms(
                        lambda c, aoh=aoh: aoh(c),
                        lambda c: sqb[:, c * 256:(c + 1) * 256],
                        lambda c: [sq_done, T['psO2_free']] if c == 0 else [],
                        None, ps_m=psE2[0:1, 256:512], ps_q=psO2[0:1, 256:512])
                    T['sqb_free'] = stats2
                    T['aob_free'] = stats2
                    prev_sqb_free = stats2
                    psum_free, rows2, rst2, nm2 = rows_stats(
                        psE2[0:1, 256:512], psO2[0:1, 256:512], [stats2])
                    T['psO2_free'] = psum_free
                    bb2 = ebounce_rows(rows2, rst2, nm2, T['rsnm_free'])
                    yield
                    if DBG and b == 0 and qt == 0 and qh == 0:
                        ddump("ao", poolA[:, 0:4096], T['ao_evac'])
                    x1caps = []
                    for c in range(DCH):
                        xrsl = 7680 + (c % 2) * 256
                        em('y', [bb2, T['x2r_free'][c % 2]] if c < 2
                           else [x1caps[c - 2]],
                           lambda y, c=c, xrsl=xrsl, h0=h0, qt=qt, b=b: y.dma_start(
                               out=poolA32[:, xrsl:xrsl + 256],
                               in_=xT[b, c * 128:(c + 1) * 128,
                                      qt * 512 + h0: qt * 512 + h0 + 256]),
                           inc=I(sxr[c % 2], 16))
                        xr_ready = W(sxr[c % 2])
                        em('v', [bb2, xr_ready], lambda v, c=c, aoh=aoh: v.tensor_tensor(
                            ttmp[:, 0:256], aoh(c),
                            psE2[:, 256:512], ALU.mult))
                        em('v', [], lambda v: v.tensor_tensor(
                            ttmp[:, 256:512], ttmp[:, 0:256], psO2[:, 256:512],
                            ALU.add))
                        em('v', [], lambda v, c=c, xrsl=xrsl: v.tensor_tensor(
                            poolA32[:, 2048 + c * 256: 2048 + (c + 1) * 256],
                            ttmp[:, 256:512], poolA32[:, xrsl:xrsl + 256], ALU.add),
                            inc=I(sve))
                        x1caps.append(W(sve))
                        if c % 3 == 2:
                            yield
                    T['rsnm_free'] = W(sve)
                    yield
                    if DBG and b == 0 and qt == 0 and qh == 0:
                        ddump("x1", poolA32[:, 2048:4096], x1caps[-1])
                        ddump("r2a", rw[0:1, 256:512], rows2)
                        ddump("r2b", rw[0:1, 768:1024], rows2)
                    for c in range(DCH):
                        em('v', [T['sqb_free']] if c == 0 else [],
                           lambda v, c=c: v.tensor_tensor(
                               sqb[:, c * 256:(c + 1) * 256],
                               poolA32[:, 2048 + c * 256: 2048 + (c + 1) * 256],
                               poolA32[:, 2048 + c * 256: 2048 + (c + 1) * 256],
                               ALU.mult), inc=I(sve))
                    ln3_src = W(sve)
                    yield
                    yield
                    yield
                    if qh == 0:
                        yield
                        yield
                    stats3 = stat_mms(
                        lambda c: poolA32[:, 2048 + c * 256: 2048 + (c + 1) * 256],
                        lambda c: sqb[:, c * 256:(c + 1) * 256],
                        lambda c: [ln3_src, T['psO2_free']] if c == 0 else [],
                        None, ps_m=psE2[0:1, 256:512], ps_q=psO2[0:1, 256:512],
                        inv=invEf)
                    T['sqb_free'] = stats3
                    prev_sqb_free = stats3
                    psum_free, rows3, rst3, nm3 = rows_stats(
                        psE2[0:1, 256:512], psO2[0:1, 256:512], [stats3])
                    T['psO2_free'] = psum_free
                    bb3 = ebounce_rows(rows3, rst3, nm3, T['rsnm_free'])
                    yield
                    for c in range(DCH):
                        em('v', [bb3] if c == 0 else [], lambda v, c=c: v.tensor_tensor(
                            ttmp[:, 0:256],
                            poolA32[:, 2048 + c * 256: 2048 + (c + 1) * 256],
                            psE2[:, 256:512], ALU.mult))
                        em('v', [], lambda v, c=c: v.tensor_tensor(
                            poolA[:, 8192 + c * 256: 8192 + (c + 1) * 256],
                            ttmp[:, 0:256], psO2[:, 256:512], ALU.add), inc=I(sve))
                        if c % 3 == 2:
                            yield
                    xn1_done = W(sve)
                    T['rsnm_free'] = W(sve)
                    T['psO2_free'] = W(sve)
                    yield
                    if DBG and b == 0 and qt == 0 and qh == 0:
                        ddump("xn1", poolA[:, 8192:10240], xn1_done)
                    yield
                    yield
                    yield
                    lw_qf = load_w(ws0, "wqf", [T['ws0_free']])
                    q2caps = []
                    for c in range(DCH):
                        ps4 = P4SLOT[c % 2]
                        for d in range(DCH):
                            waits = []
                            if d == 0:
                                waits = [xn1_done, lw_qf]
                                if len(q2caps) >= 2:
                                    waits.append(q2caps[-2])
                                else:
                                    waits += [T['slot'][c % 2], T['psO2_free']]
                            em('t', waits, lambda t, c=c, d=d, ps4=ps4: t.matmul(
                                ps4, ws0[:, d * 1024 + c * 128: d * 1024 + (c + 1) * 128],
                                poolA[:, 8192 + d * 256: 8192 + (d + 1) * 256],
                                start=(d == 0), stop=(d == DCH - 1),
                                skip_group_check=True),
                                inc=I(smm) if d == DCH - 1 else None)
                        mmq = W(smm)
                        em('v', [mmq], lambda v, c=c, ps4=ps4: v.tensor_copy(
                            poolA[:, 10240 + c * 256: 10240 + (c + 1) * 256], ps4),
                            inc=I(sve))
                        q2caps.append(W(sve))
                        T['slot'][c % 2] = q2caps[-1]
                        if c % 2 == 1:
                            yield
                    q2_done = q2caps[-1]
                    if DBG and b == 0 and qt == 0 and qh == 0:
                        ddump("q2", poolA[:, 10240:12288], q2_done)
                    T['ws0_free'] = W(smm)
                    sp_caps = []
                    for rc in range(2):
                        ps4 = P4SLOT[rc % 2]
                        for d in range(DCH):
                            waits = []
                            if d == 0:
                                waits = [q2_done, ffn_tab, T['slot'][rc % 2]]
                            em('t', waits, lambda t, rc=rc, d=d, ps4=ps4: t.matmul(
                                ps4, tab[:, 2048 + d * 256 + rc * 128:
                                         2048 + d * 256 + rc * 128 + 128],
                                poolA[:, 10240 + d * 256: 10240 + (d + 1) * 256],
                                start=(d == 0), stop=(d == DCH - 1),
                                skip_group_check=True),
                                inc=I(smm) if d == DCH - 1 else None)
                        mmz = W(smm)
                        e1 = poolA32[:, 7424:7680]
                        for sgn, off in ((-1.0, 0), (1.0, 2)):
                            em('s', [mmz, W(sex)], lambda s, ps4=ps4, sgn=sgn, e1=e1:
                               s.activation(e1, ps4, AF.Exp, scale=sgn), inc=I(sex))
                            ex1 = W(sex)
                            em('s', [ex1], lambda s, rc=rc, off=off, e1=e1: s.activation(
                                poolA32[:, 6144 + (off + rc) * 256:
                                        6144 + (off + rc + 1) * 256],
                                e1, AF.Ln, bias=onef[:, 0:1]), inc=I(sex))
                        sp_caps.append(W(sex))
                        T['slot'][rc % 2] = sp_caps[-1]
                        yield
                    sp_done = sp_caps[-1]
                    T['q2_free'] = W(smm)
                    yield
                    lp_caps = []
                    for lc in range(2):
                        ps4 = P4SLOT[lc % 2]
                        for kk in range(4):
                            waits = []
                            if kk == 0:
                                waits = [sp_done, const_x, T['slot'][lc % 2]]
                            em('t', waits, lambda t, lc=lc, kk=kk, ps4=ps4: t.matmul(
                                ps4, bmf[:, kk * 256 + lc * 128: kk * 256 + lc * 128 + 128],
                                poolA32[:, 6144 + kk * 256: 6144 + (kk + 1) * 256],
                                start=(kk == 0), stop=(kk == 3),
                                skip_group_check=True),
                                inc=I(smm) if kk == 3 else None)
                        mml = W(smm)
                        em('s', [mml], lambda s, lc=lc, ps4=ps4: s.activation(
                            poolA[:, 14336 + lc * 256: 14336 + (lc + 1) * 256],
                            ps4, AF.Exp), inc=I(sex))
                        lp_caps.append(W(sex))
                        T['slot'][lc % 2] = lp_caps[-1]
                    lp_done = lp_caps[-1]
                    if DBG and b == 0 and qt == 0 and qh == 0:
                        ddump("lg", poolA32[:, 6144:7168], sp_done)
                        ddump("lp", poolA[:, 14336:14848], lp_done)
                    yield
                    yield
                    lw_wout = load_w(ws0, "wout", [T['ws0_free']])
                    retcaps = []
                    for c in range(DCH):
                        ps4 = P4SLOT[c % 2]
                        for lc in range(2):
                            waits = []
                            if lc == 0:
                                waits = [lp_done, T['q2_free']]
                                waits.append(retcaps[-2] if len(retcaps) >= 2
                                             else T['slot'][c % 2])
                            em('t', waits, lambda t, c=c, lc=lc, ps4=ps4: t.matmul(
                                ps4, tab[:, lc * 1024 + c * 128: lc * 1024 + (c + 1) * 128],
                                poolA[:, 14336 + lc * 256: 14336 + (lc + 1) * 256],
                                start=(lc == 0), stop=(lc == 1),
                                skip_group_check=True),
                                inc=I(smm) if lc == 1 else None)
                        mmr = W(smm)
                        em('v', [mmr], lambda v, c=c, ps4=ps4: v.tensor_copy(
                            poolA[:, 10240 + c * 256: 10240 + (c + 1) * 256], ps4),
                            inc=I(sve))
                        retcaps.append(W(sve))
                        T['slot'][c % 2] = retcaps[-1]
                        if c % 4 == 3:
                            yield
                    ret_done = retcaps[-1]
                    if DBG and b == 0 and qt == 0 and qh == 0:
                        ddump("rt", poolA[:, 10240:12288], ret_done)
                    o2caps = []
                    for c in range(DCH):
                        ps4 = P4SLOT[c % 2]
                        for d in range(DCH):
                            waits = []
                            if d == 0:
                                waits = [ret_done, lw_wout]
                                waits.append(o2caps[-2][1] if len(o2caps) >= 2
                                             else T['slot'][c % 2])
                            em('t', waits, lambda t, c=c, d=d, ps4=ps4: t.matmul(
                                ps4, ws0[:, d * 1024 + c * 128: d * 1024 + (c + 1) * 128],
                                poolA[:, 10240 + d * 256: 10240 + (d + 1) * 256],
                                start=(d == 0), stop=(d == DCH - 1),
                                skip_group_check=True),
                                inc=I(smm) if d == DCH - 1 else None)
                        mmo = W(smm)
                        em('v', [mmo, T['aob_free'] if c == 0 else (smm.h, 0)],
                           lambda v, c=c, ps4=ps4: v.tensor_scalar(
                               aob(c), ps4, wbF[:, c:c + 1], None, ALU.add),
                           inc=I(sve))
                        em('v', [T['sqb_free'] if c == 0 else (smm.h, 0)],
                           lambda v, c=c: v.tensor_tensor(
                               sqb[:, c * 256:(c + 1) * 256], aob(c), aob(c),
                               ALU.mult), inc=I(sve))
                        o2caps.append((W(smm), W(sve)))
                        T['slot'][c % 2] = o2caps[-1][1]
                        if c % 2 == 1:
                            yield
                    if DBG and b == 0 and qt == 0 and qh == 0:
                        ddump("o2", poolA32[:, 0:2048], o2caps[-1][1])
                    last_out2_mm = W(smm)
                    T['ws0_free'] = last_out2_mm
                    yield
                    yield
                    yield
                    stats4 = stat_mms(
                        lambda c: aob(c),
                        lambda c: sqb[:, c * 256:(c + 1) * 256],
                        lambda c: [o2caps[-1][1], T['psO2_free']] if c == 0 else [],
                        None, ps_m=psE2[0:1, 256:512], ps_q=psO2[0:1, 256:512])
                    T['sqb_free'] = stats4
                    T['aob_free'] = stats4
                    prev_sqb_free = stats4
                    psum_free, rows4, rst4, nm4 = rows_stats(
                        psE2[0:1, 256:512], psO2[0:1, 256:512], [stats4])
                    T['psO2_free'] = psum_free
                    bb4 = ebounce_rows(rows4, rst4, nm4, T['rsnm_free'])
                    yield
                    st_caps = []
                    for c in range(DCH):
                        x2sl = 7680 + (c % 2) * 256
                        em('v', [bb4] if c == 0 else [],
                           lambda v, c=c: v.tensor_tensor(
                               ttmp[:, 0:256], aob(c),
                               psE2[:, 256:512], ALU.mult))
                        em('v', [], lambda v: v.tensor_tensor(
                            ttmp[:, 256:512], ttmp[:, 0:256], psO2[:, 256:512],
                            ALU.add))
                        em('v', [st_caps[c - 2]] if c >= 2 else [],
                           lambda v, c=c, x2sl=x2sl: v.tensor_tensor(
                               poolA32[:, x2sl:x2sl + 256], ttmp[:, 256:512],
                               poolA32[:, 2048 + c * 256: 2048 + (c + 1) * 256],
                               ALU.add), inc=I(sve))
                        x2done = W(sve)
                        if c % 3 == 2:
                            yield
                        em('y', [x2done], lambda y, c=c, x2sl=x2sl, qt=qt, h0=h0, b=b:
                           y.dma_start(out=oT[b, c * 128:(c + 1) * 128,
                                             qt * 512 + h0: qt * 512 + h0 + 256],
                                       in_=poolA32[:, x2sl:x2sl + 256]),
                           inc=I(sstp[c % 2], 16))
                        st_caps.append(W(sstp[c % 2]))
                        T['x2r_free'][c % 2] = W(sstp[c % 2])
                    T['rsnm_free'] = W(sve)
                    T['psO2_free'] = W(sve)
                    yield

            _done = object()

            def pump(n=1):
                for _ in range(n):
                    while pend:
                        if next(pend[0], _done) is _done:
                            pend.pop(0)
                            continue
                        break
                    if not pend:
                        return

            for qt in range(NQT):
                gqt = b * NQT + qt
                while pend and T['op_flag'].get(gqt - 2, True) is False:
                    pump(1)
                mid = middle_gen(qt)
                for si, _ in enumerate(mid):
                    if si >= 12:
                        pump(2 if len(pend) > 1 else 1)
                T['op_flag'][gqt] = False
                pend.append(epilogue_gen(qt))
            while pend:
                for _ in pend.pop(0):
                    pass

        fin0, fin1 = W(sst), W(sst1)

        def _final(y, fin0=fin0, fin1=fin1):
            y.wait_ge(fin0[0], fin0[1])
            y.wait_ge(fin1[0], fin1[1])
        P['y'].append(_final)

        with nc.Block() as block:
            @block.tensor
            def _(t):
                for f in P['t']:
                    f(t)

            @block.vector
            def _(v):
                for f in P['v']:
                    f(v)

            @block.scalar
            def _(s):
                for f in P['s']:
                    f(s)

            @block.gpsimd
            def _(g):
                for f in P['g']:
                    f(g)

            @block.sync
            def _(y):
                for f in P['y']:
                    f(y)
    return nc


def _host_inputs(L0_attn_x, L0_attn_Wq, L0_attn_bq, L0_attn_Wk, L0_attn_bk,
                 L0_attn_Wv, L0_attn_bv, L0_attn_Wo, L0_attn_bo, L0_ffn_Wq,
                 L0_ffn_routers, L0_ffn_leaf_keys, L0_ffn_leaf_values,
                 L0_ffn_Wout_w, L0_ffn_Wout_b):
    f = np.float32
    conT = lambda a: np.ascontiguousarray(np.asarray(a), dtype=f)
    wq = conT(np.asarray(L0_attn_Wq).T)
    wk = conT(np.asarray(L0_attn_Wk).T)
    wv = conT(np.asarray(L0_attn_Wv).T)
    wo = conT(np.asarray(L0_attn_Wo).T)
    wqf = conT(np.asarray(L0_ffn_Wq).T)
    wout = conT(np.asarray(L0_ffn_Wout_w).T)
    routersT = np.zeros((E, 256), f)
    routersT[:, 0:255] = np.asarray(L0_ffn_routers).T
    lvh = conT(L0_ffn_leaf_values)
    bias8 = lambda v: conT(np.asarray(v).reshape(8, 128).T)
    bq, bk, bo, wb = (bias8(L0_attn_bq), bias8(L0_attn_bk),
                      bias8(L0_attn_bo), bias8(L0_ffn_Wout_b))
    bv = conT(np.asarray(L0_attn_bv).reshape(1, E))
    invf = 1.0 / 10000.0 ** (np.arange(0, 64, 2, dtype=np.float64) / 64.0)
    j = np.arange(64)
    freq_j = invf[j // 2]
    t = np.arange(L, dtype=np.float64)
    ang = freq_j[:, None] * t[None, :]
    cos64 = np.cos(ang)
    sin64 = np.sin(ang)
    sgn = np.where(j % 2 == 0, -1.0, 1.0)[:, None]
    cosT = conT(np.tile(cos64, (2, 1)))
    sinT = conT(np.tile(sin64 * sgn, (2, 1)))
    pswap = np.zeros((128, 128), f)
    idx = np.arange(128)
    pswap[idx, idx ^ 1] = 1.0
    bmat = np.zeros((512, 256), f)
    for lev in range(8):
        n = 1 << lev
        for m in range(256):
            i = m >> (8 - lev)
            c = (m >> (7 - lev)) & 1
            node = n - 1 + i
            if c:
                bmat[node, m] = -1.0
            else:
                bmat[256 + node, m] = -1.0
    x = np.asarray(L0_attn_x, dtype=f)
    shards = []
    for core in range(8):
        xs = np.ascontiguousarray(
            np.transpose(x[2 * core:2 * core + 2], (0, 2, 1)), dtype=f)
        shards.append({
            "xT": xs, "wq": wq, "wk": wk, "wv": wv, "wo": wo, "wqf": wqf,
            "wout": wout, "routersT": routersT, "lv": lvh, "bmat": bmat,
            "bq": bq, "bk": bk, "bo": bo, "wb": wb, "bv": bv,
            "cosT": cosT, "sinT": sinT, "pswap": pswap,
        })
    return shards


def kernel(**inputs):
    shards = _host_inputs(**inputs)
    nc = build()
    res = run_bass_kernel_spmd(nc, shards, core_ids=list(range(8)))
    out = np.empty((16, 2048, 1024), np.float32)
    for core in range(8):
        o = res.results[core]["oT"]
        out[2 * core] = o[0].T
        out[2 * core + 1] = o[1].T
    return out
